# version 12
# baseline (speedup 1.0000x reference)
import sys
import contextlib

sys.path.insert(0, "/opt/trn_rl_repo")

import numpy as np
import ml_dtypes

import concourse.bass as bass
import concourse.mybir as mybir
from concourse.bass_utils import run_bass_kernel_spmd

import concourse.bass_utils as _bu

if not getattr(_bu, "_ldwopt_patched", False):
    _orig_run_command = _bu.run_command

    def _run_command_ldwopt(cmd, *a, **kw):
        try:
            cmd = [("--enable-ldw-opt=true" if c == "--enable-ldw-opt=false" else c)
                   for c in cmd]
        except TypeError:
            pass
        return _orig_run_command(cmd, *a, **kw)

    _bu.run_command = _run_command_ldwopt
    _bu._ldwopt_patched = True


F32 = mybir.dt.float32
BF16 = mybir.dt.bfloat16
I32 = mybir.dt.int32
AF = mybir.ActivationFunctionType
ALU = mybir.AluOpType
AX = mybir.AxisListType
BF = ml_dtypes.bfloat16

B = 2
S = 4096
D = 2048
H = 16
HD = 128
NCORES = 8
T = (B * S) // NCORES
NT = T // 128
NPAIR = (B * H) // NCORES
EPS = 1e-5
ISQ = float(1.0 / np.sqrt(128.0))
NEG = -1.0e30

_cache = {}

ENGS = ("gpsimd", "vector", "scalar", "tensor", "sync")


class Plan:

    def __init__(self):
        self.ops = {e: [] for e in ENGS}

    def add(self, eng, fn):
        self.ops[eng].append(fn)

    def run(self, nc):
        with nc.Block() as block:
            @block.gpsimd
            def _(g):
                for f in self.ops["gpsimd"]:
                    f(g)

            @block.vector
            def _(v):
                for f in self.ops["vector"]:
                    f(v)

            @block.scalar
            def _(s):
                for f in self.ops["scalar"]:
                    f(s)

            @block.tensor
            def _(t):
                for f in self.ops["tensor"]:
                    f(t)

            @block.sync
            def _(sy):
                for f in self.ops["sync"]:
                    f(sy)


class Ctr:
    def __init__(self):
        self.v = 0

    def inc(self, n=1):
        self.v += n
        return self.v



class SlotSem:

    def __init__(self, es, nc, name):
        self.sem = es.enter_context(nc.semaphore(name))
        self.c = Ctr()

    def inc(self):
        return self.c.inc(16)

    def total(self):
        return self.c.v


def W(p, eng, sem, val):
    if val > 0:
        p.add(eng, lambda e, sem=sem, val=val: e.wait_ge(sem, val))



def build_l1():
    nc = bass.Bass("TRN2", target_bir_lowering=False, debug=False)
    x_sl = nc.dram_tensor("x_sl", [T, D], F32, kind="ExternalInput").ap()
    wa = nc.dram_tensor("wa", [32, 128, 16 * 128], F32, kind="ExternalInput").ap()
    wv = nc.dram_tensor("wv", [16, 128, D], F32, kind="ExternalInput").ap()
    ident_in = nc.dram_tensor("ident_in", [128, 128], BF16, kind="ExternalInput").ap()
    q_out = nc.dram_tensor("q_out", [D, T], BF16, kind="ExternalOutput").ap()
    k_out = nc.dram_tensor("k_out", [D, T], BF16, kind="ExternalOutput").ap()
    v_out = nc.dram_tensor("v_out", [T, D], BF16, kind="ExternalOutput").ap()

    es = contextlib.ExitStack()
    with es:
        xt = [es.enter_context(nc.sbuf_tensor(f"xt{i}", [128, D], F32)) for i in range(2)]
        stat = es.enter_context(nc.sbuf_tensor("stat", [128, 16], F32))
        ceps = es.enter_context(nc.sbuf_tensor("ceps", [128, 1], F32))
        xn2t = [es.enter_context(nc.sbuf_tensor(f"xn2t{i}", [128, D], BF16)) for i in range(2)]
        xn_fm = es.enter_context(nc.sbuf_tensor("xn_fm", [128, 16 * T], BF16))
        ident = es.enter_context(nc.sbuf_tensor("ident", [128, 128], BF16))
        wqk = [es.enter_context(nc.sbuf_tensor(f"wqk{i}", [128, 2048], BF16)) for i in range(2)]
        wvb = es.enter_context(nc.sbuf_tensor("wvb", [128, 16 * D], BF16))
        stg = [es.enter_context(nc.sbuf_tensor(f"stg{i}", [128, T], BF16)) for i in range(2)]
        vstg = [es.enter_context(nc.sbuf_tensor(f"vstg{i}", [128, D], BF16)) for i in range(2)]
        junk = es.enter_context(nc.sbuf_tensor("junk", [128, D], BF16))
        ptr = [es.enter_context(nc.psum_tensor(f"ptr{i}", [128, 128], BF16)) for i in range(2)]
        pmm = [es.enter_context(nc.psum_tensor(f"pmm{i}", [128, 512], F32)) for i in range(4)]

        lc = SlotSem(es, nc, "lc")
        lx0 = SlotSem(es, nc, "lx0")
        lx = [lx0, lx0]
        lv = SlotSem(es, nc, "lv")
        lw = [SlotSem(es, nc, f"lw{i}") for i in range(2)]
        sq = [SlotSem(es, nc, f"sq{i}") for i in range(2)]
        sv = [SlotSem(es, nc, f"sv{i}") for i in range(2)]
        ve = es.enter_context(nc.semaphore("ve"))
        se = es.enter_context(nc.semaphore("se"))
        te = es.enter_context(nc.semaphore("te"))
        cve, cse, cte = Ctr(), Ctr(), Ctr()

        p = Plan()

        p.add("gpsimd", lambda g: g.dma_start(out=ident[:], in_=ident_in).then_inc(lc.sem, 16))
        ld_ident = lc.inc()
        p.add("vector", lambda v: v.memset(ceps[:], EPS).then_inc(ve, 1))
        cve.inc()

        ve_trev = {}
        astate = {}
        xt_guard = {0: 0, 1: 0}
        xn_guard = {0: 0, 1: 0}
        sq_guard = {0: 0, 1: 0}

        def a_stageA(tt):
            sl = tt % 2
            st_ = {}
            if xt_guard[sl]:
                W(p, "gpsimd", ve, xt_guard[sl])
            p.add("gpsimd", lambda g, tt=tt, sl=sl: g.dma_start(
                out=xt[sl][:], in_=x_sl[tt * 128:(tt + 1) * 128, :]).then_inc(lx[sl].sem, 16))
            ld_x = lx[sl].inc()
            st_["ld_x"] = ld_x
            W(p, "vector", lx[sl].sem, ld_x)
            p.add("vector", lambda v, sl=sl: v.tensor_reduce(
                stat[:, sl:sl + 1], xt[sl][:], AX.X, ALU.add).then_inc(ve, 1))
            st_["ve_red"] = cve.inc()
            W(p, "scalar", lx[sl].sem, ld_x)
            if sq_guard[sl]:
                W(p, "scalar", ve, sq_guard[sl])
            p.add("scalar", lambda s, sl=sl: s.activation(
                junk[:], xt[sl][:], AF.Square, accum_out=stat[:, 2 + sl:3 + sl]).then_inc(se, 1))
            st_["se_sq"] = cse.inc()
            astate[tt] = st_

        def a_stageB(tt):
            sl = tt % 2
            st_ = astate[tt]
            W(p, "vector", ve, st_["ve_red"])
            W(p, "vector", se, st_["se_sq"])
            p.add("vector", lambda v, sl=sl: v.tensor_scalar(
                stat[:, 4 + sl:5 + sl], stat[:, sl:sl + 1], 1.0 / D, None,
                ALU.mult).then_inc(ve, 1))
            ve_m = cve.inc()
            W(p, "vector", ve, ve_m)
            p.add("vector", lambda v, sl=sl: v.tensor_tensor(
                stat[:, 6 + sl:7 + sl], stat[:, 4 + sl:5 + sl], stat[:, 4 + sl:5 + sl],
                ALU.mult).then_inc(ve, 1))
            ve_m2 = cve.inc()
            W(p, "vector", ve, ve_m2)
            p.add("vector", lambda v, sl=sl: v.tensor_scalar(
                stat[:, 8 + sl:9 + sl], stat[:, 2 + sl:3 + sl], 1.0 / D,
                stat[:, 6 + sl:7 + sl], ALU.mult, ALU.subtract).then_inc(ve, 1))
            ve_var = cve.inc()
            sq_guard[sl] = ve_var
            W(p, "scalar", ve, ve_var)
            p.add("scalar", lambda s, sl=sl: s.activation(
                stat[:, 10 + sl:11 + sl], stat[:, 8 + sl:9 + sl], AF.Ln,
                bias=ceps[:]).then_inc(se, 1))
            se_ln = cse.inc()
            W(p, "scalar", se, se_ln)
            p.add("scalar", lambda s, sl=sl: s.activation(
                stat[:, 12 + sl:13 + sl], stat[:, 10 + sl:11 + sl], AF.Exp,
                scale=-0.5).then_inc(se, 1))
            se_rs = cse.inc()
            W(p, "vector", se, se_rs)
            if xn_guard[sl]:
                W(p, "vector", te, xn_guard[sl])
            p.add("vector", lambda v, sl=sl: v.tensor_scalar(
                xn2t[sl][:], xt[sl][:], stat[:, 4 + sl:5 + sl], stat[:, 12 + sl:13 + sl],
                ALU.subtract, ALU.mult).then_inc(ve, 1))
            st_["ve_xn"] = cve.inc()
            xt_guard[sl] = st_["ve_xn"]

        def a_stageC(tt):
            sl = tt % 2
            st_ = astate[tt]
            W(p, "tensor", lc.sem, ld_ident)
            W(p, "tensor", ve, st_["ve_xn"])
            te_tr_last = 0
            for dc in range(16):
                bk = dc % 2
                prev = (tt, dc - 2) if dc >= 2 else ((tt - 1, 14 + dc) if tt >= 1 else None)
                if prev is not None and prev in ve_trev:
                    W(p, "tensor", ve, ve_trev[prev])
                p.add("tensor", lambda t, tt=tt, dc=dc, sl=sl, bk=bk: t.transpose(
                    ptr[bk][:], xn2t[sl][:, dc * 128:(dc + 1) * 128], ident[:]).then_inc(te, 1))
                te_tr = cte.inc()
                te_tr_last = te_tr
                W(p, "vector", te, te_tr)
                p.add("vector", lambda v, tt=tt, dc=dc, bk=bk: v.tensor_copy(
                    xn_fm[:, dc * T + tt * 128: dc * T + (tt + 1) * 128],
                    ptr[bk][:]).then_inc(ve, 1))
                ve_trev[(tt, dc)] = cve.inc()
            xn_guard[sl] = te_tr_last
            del astate[tt]

        for step in range(NT + 2):
            if step >= 2:
                a_stageC(step - 2)
            if 1 <= step <= NT:
                a_stageB(step - 1)
            if step < NT:
                a_stageA(step)
        ve_phaseA = cve.v

        for i in range(16):
            p.add("gpsimd", lambda g, i=i: g.dma_start(
                out=wvb[:, i * D:(i + 1) * D], in_=wv[i]).then_inc(lv.sem, 16))
            lv.inc()
        ld_wv = lv.total()

        te_qk = {}
        se_evac = {}
        so_qk = {}
        bankprev = {}
        for oc in range(32):
            wsl = oc % 2
            if oc >= 2:
                W(p, "gpsimd", te, te_qk[oc - 2])
            p.add("gpsimd", lambda g, oc=oc, wsl=wsl: g.dma_start(
                out=wqk[wsl][:], in_=wa[oc]).then_inc(lw[wsl].sem, 16))
            ld_w = lw[wsl].inc()
            for tg in range(2):
                bank = (2 * oc + tg) % 4
                W(p, "tensor", lw[wsl].sem, ld_w)
                W(p, "tensor", ve, ve_phaseA)
                if bank in bankprev:
                    W(p, "tensor", se, bankprev[bank])
                for ic in range(16):
                    def mmqk(t, oc=oc, tg=tg, ic=ic, bank=bank, wsl=wsl):
                        ins = t.matmul(
                            pmm[bank][:], wqk[wsl][:, ic * 128:(ic + 1) * 128],
                            xn_fm[:, ic * T + tg * 512: ic * T + tg * 512 + 512],
                            start=(ic == 0), stop=(ic == 15), skip_group_check=True)
                        if ic == 15:
                            ins.then_inc(te, 1)
                    p.add("tensor", mmqk)
                te_mm = cte.inc()
                if tg == 1:
                    te_qk[oc] = te_mm
                W(p, "scalar", te, te_mm)
                if oc >= 2 and tg == 0:
                    W(p, "scalar", sq[wsl].sem, so_qk[oc - 2])
                p.add("scalar", lambda s, oc=oc, tg=tg, bank=bank, wsl=wsl: s.activation(
                    stg[wsl][:, tg * 512:tg * 512 + 512], pmm[bank][:], AF.Copy,
                    scale=(ISQ if oc < 16 else 1.0)).then_inc(se, 1))
                se_evac[(oc, tg)] = cse.inc()
                bankprev[bank] = se_evac[(oc, tg)]
            W(p, "sync", se, se_evac[(oc, 1)])
            dst = q_out if oc < 16 else k_out
            r0 = (oc % 16) * 128
            p.add("sync", lambda sy, oc=oc, dst=dst, r0=r0, wsl=wsl: sy.dma_start(
                out=dst[r0:r0 + 128, :], in_=stg[wsl][:]).then_inc(sq[wsl].sem, 16))
            so_qk[oc] = sq[wsl].inc()

        ve_vevac = {}
        so_v = {}
        for tt in range(NT):
            vsl = tt % 2
            for og in range(4):
                bank = og
                W(p, "tensor", lv.sem, ld_wv)
                if tt == 0:
                    W(p, "tensor", se, bankprev[bank])
                prevkey = (tt - 1, og)
                if prevkey in ve_vevac:
                    W(p, "tensor", ve, ve_vevac[prevkey])
                for ic in range(16):
                    def mmv(t, tt=tt, og=og, ic=ic, bank=bank):
                        ins = t.matmul(
                            pmm[bank][:], xn_fm[:, ic * T + tt * 128: ic * T + (tt + 1) * 128],
                            wvb[:, ic * D + og * 512: ic * D + og * 512 + 512],
                            start=(ic == 0), stop=(ic == 15), skip_group_check=True)
                        if ic == 15:
                            ins.then_inc(te, 1)
                    p.add("tensor", mmv)
                te_v = cte.inc()
                W(p, "vector", te, te_v)
                if tt >= 2 and og == 0:
                    W(p, "vector", sv[vsl].sem, so_v[tt - 2])
                p.add("vector", lambda v, tt=tt, og=og, bank=bank, vsl=vsl: v.tensor_copy(
                    vstg[vsl][:, og * 512:og * 512 + 512], pmm[bank][:]).then_inc(ve, 1))
                ve_vevac[(tt, og)] = cve.inc()
            W(p, "sync", ve, ve_vevac[(tt, 3)])
            p.add("sync", lambda sy, tt=tt, vsl=vsl: sy.dma_start(
                out=v_out[tt * 128:(tt + 1) * 128, :], in_=vstg[vsl][:]).then_inc(sv[vsl].sem, 16))
            so_v[tt] = sv[vsl].inc()

        for ss in (sq[0], sq[1], sv[0], sv[1]):
            W(p, "sync", ss.sem, ss.total())

        p.run(nc)
    return nc



def l1_weight_layout(W_qkv):
    wqk = W_qkv[:, :2 * D]
    wa = np.ascontiguousarray(
        wqk.reshape(16, 128, 32, 128).transpose(2, 1, 0, 3).reshape(32, 128, 16 * 128))
    wv = np.ascontiguousarray(W_qkv[:, 2 * D:].reshape(16, 128, D))
    return wa, wv


IDENT_BF = np.eye(128, dtype=np.float32).astype(BF)



def build_l2():
    nc = bass.Bass("TRN2", target_bir_lowering=False, debug=False)
    qs_d = nc.dram_tensor("qs_d", [NPAIR, 128, S], BF16, kind="ExternalInput").ap()
    kk_d = nc.dram_tensor("kk_d", [NPAIR, 128, S], BF16, kind="ExternalInput").ap()
    vt_d = nc.dram_tensor("vt_d", [NPAIR, 128, S], BF16, kind="ExternalInput").ap()
    mk_d = nc.dram_tensor("mk_d", [128, 4 * 512], BF16, kind="ExternalInput").ap()
    id_d = nc.dram_tensor("id_d", [128, 128], BF16, kind="ExternalInput").ap()
    avn_d = nc.dram_tensor("avn_d", [NPAIR, 128, S], BF16, kind="ExternalOutput").ap()

    NKT = S // 128
    NQC = S // 512
    es = contextlib.ExitStack()
    with es:
        qb = [es.enter_context(nc.sbuf_tensor(f"qb{i}", [128, S], BF16)) for i in range(NPAIR)]
        kb = [es.enter_context(nc.sbuf_tensor(f"kb{i}", [128, S], BF16)) for i in range(NPAIR)]
        vb = [es.enter_context(nc.sbuf_tensor(f"vb{i}", [128, S], BF16)) for i in range(NPAIR)]
        mkb = es.enter_context(nc.sbuf_tensor("mkb", [128, 4 * 512], BF16))
        idb = es.enter_context(nc.sbuf_tensor("idb", [128, 128], BF16))
        ones_bf = es.enter_context(nc.sbuf_tensor("ones_bf", [128, 1], BF16))
        ones_f = es.enter_context(nc.sbuf_tensor("ones_f", [1, 128], F32))
        at = es.enter_context(nc.sbuf_tensor("at", [128, 4 * 512], BF16))
        sq2 = [es.enter_context(nc.sbuf_tensor(f"sq2_{i}", [128, 512], BF16)) for i in range(2)]
        nrow = es.enter_context(nc.sbuf_tensor("nrow", [1, 2 * NQC * 512], F32))
        mrow = es.enter_context(nc.sbuf_tensor("mrow", [1, 8], F32))
        negm = es.enter_context(nc.sbuf_tensor("negm", [128, NPAIR], F32))
        rz = [es.enter_context(nc.sbuf_tensor(f"rz{i}", [1, 512], F32)) for i in range(2)]
        rzb = es.enter_context(nc.sbuf_tensor("rzb", [128, 512], F32))
        avs = [es.enter_context(nc.sbuf_tensor(f"avs{i}", [128, 512], BF16)) for i in range(2)]
        ps = [es.enter_context(nc.psum_tensor(f"ps{i}", [128, 512], F32)) for i in range(2)]
        pav = [es.enter_context(nc.psum_tensor(f"pav{i}", [128, 512], F32)) for i in range(2)]
        pz = [es.enter_context(nc.psum_tensor(f"pz{i}", [1, 512], F32)) for i in range(2)]
        przb = es.enter_context(nc.psum_tensor("przb", [128, 512], F32))
        pn2 = es.enter_context(nc.psum_tensor("pn2", [1, 512], F32))

        lc = SlotSem(es, nc, "lc")
        sa = [SlotSem(es, nc, f"sa{i}") for i in range(2)]
        ve = es.enter_context(nc.semaphore("ve"))
        se = es.enter_context(nc.semaphore("se"))
        te = es.enter_context(nc.semaphore("te"))
        cve, cse, cte = Ctr(), Ctr(), Ctr()
        p = Plan()

        p.add("gpsimd", lambda g: g.dma_start(out=mkb[:], in_=mk_d).then_inc(lc.sem, 16))
        lc.inc()
        p.add("gpsimd", lambda g: g.dma_start(out=idb[:], in_=id_d).then_inc(lc.sem, 16))
        lc.inc()
        for pp in range(NPAIR):
            p.add("gpsimd", lambda g, pp=pp: g.dma_start(out=qb[pp][:], in_=qs_d[pp]).then_inc(lc.sem, 16))
            lc.inc()
            p.add("gpsimd", lambda g, pp=pp: g.dma_start(out=kb[pp][:], in_=kk_d[pp]).then_inc(lc.sem, 16))
            lc.inc()
            p.add("gpsimd", lambda g, pp=pp: g.dma_start(out=vb[pp][:], in_=vt_d[pp]).then_inc(lc.sem, 16))
            lc.inc()
        lc_all = lc.total()
        p.add("vector", lambda v: v.memset(ones_bf[:], 1.0).then_inc(ve, 1))
        ve_ones = cve.inc()
        p.add("vector", lambda v: v.memset(ones_f[:], 1.0).then_inc(ve, 1))
        ve_onesf = cve.inc()

        se_sq_n = 0
        ve_nev_prev = 0
        te_pn_hist = {}
        ve_cp_hist = {}
        pnb = [przb[0:1, 0:512], pn2[0:1, 0:512]]
        for pp in range(NPAIR):
            for which, buf in ((0, qb[pp]), (1, kb[pp])):
                for qg in range(NQC):
                    i = se_sq_n
                    sl = i % 2
                    if i == 0:
                        W(p, "scalar", lc.sem, lc_all)
                    if i >= 2:
                        W(p, "scalar", te, te_pn_hist[i - 2])
                    p.add("scalar", lambda s, buf=buf, qg=qg, sl=sl: s.activation(
                        sq2[sl][:], buf[:, qg * 512:(qg + 1) * 512], AF.Square).then_inc(se, 1))
                    se_sq = cse.inc()
                    se_sq_n += 1
                    W(p, "tensor", se, se_sq)
                    if i == 0:
                        W(p, "tensor", ve, ve_ones)
                        W(p, "tensor", lc.sem, lc_all)
                    if i >= 2:
                        W(p, "tensor", ve, ve_cp_hist[i - 2])
                    if which == 0 and qg == 0 and ve_nev_prev:
                        W(p, "tensor", ve, ve_nev_prev)
                    p.add("tensor", lambda t, sl=sl: t.matmul(
                        pnb[sl], ones_bf[:], sq2[sl][:], start=True, stop=True,
                        skip_group_check=True).then_inc(te, 1))
                    te_pn = cte.inc()
                    te_pn_hist[i] = te_pn
                    W(p, "vector", te, te_pn)
                    col = (which * NQC + qg) * 512
                    p.add("vector", lambda v, col=col, sl=sl: v.tensor_copy(
                        nrow[:, col:col + 512], pnb[sl]).then_inc(ve, 1))
                    ve_nev = cve.inc()
                    ve_cp_hist[i] = ve_nev
                    ve_nev_prev = ve_nev
            W(p, "vector", ve, ve_nev_prev)
            p.add("vector", lambda v: v.tensor_reduce(
                mrow[:, 0:1], nrow[:, 0:NQC * 512], AX.X, ALU.max).then_inc(ve, 1))
            cve.inc()
            p.add("vector", lambda v: v.tensor_reduce(
                mrow[:, 1:2], nrow[:, NQC * 512:], AX.X, ALU.max).then_inc(ve, 1))
            ve_mk2 = cve.inc()
            W(p, "vector", ve, ve_mk2)
            p.add("vector", lambda v: v.tensor_tensor(
                mrow[:, 2:3], mrow[:, 0:1], mrow[:, 1:2], ALU.mult).then_inc(ve, 1))
            ve_mm = cve.inc()
            W(p, "scalar", ve, ve_mm)
            p.add("scalar", lambda s: s.activation(
                mrow[:, 3:4], mrow[:, 2:3], AF.Sqrt).then_inc(se, 1))
            se_m = cse.inc()
            W(p, "vector", se, se_m)
            p.add("vector", lambda v: v.tensor_scalar(
                mrow[:, 4:5], mrow[:, 3:4], -1.02, None, ALU.mult).then_inc(ve, 1))
            ve_negm = cve.inc()
            W(p, "tensor", ve, ve_negm)
            W(p, "tensor", ve, ve_onesf)
            p.add("tensor", lambda t: t.matmul(
                przb[:, 0:1], ones_f[:], mrow[:, 4:5], start=True, stop=True,
                skip_group_check=True).then_inc(te, 1))
            te_bc = cte.inc()
            W(p, "vector", te, te_bc)
            p.add("vector", lambda v, pp=pp: v.tensor_copy(
                negm[:, pp:pp + 1], przb[:, 0:1]).then_inc(ve, 1))
            ve_negm2 = cve.inc()
            ve_nev_prev = ve_negm2
        ve_prelude = cve.v

        slot_free = {}
        s_bankfree = {0: 0, 1: 0}
        bankfree = {}
        so_avs = {}
        gqc = 0
        skt = 0
        pending_tail = [None]
        pending_epi = [None]

        for pp in range(NPAIR):
            for qc in range(NQC):
                nkt = 4 * qc + 4
                pavb = pav[gqc % 2]
                pzb = pz[gqc % 2]
                events = {}

                def emit_S(kt, pp=None, qc=None):
                    nonlocal skt
                    sbank = skt % 2
                    events[("sbank", kt)] = sbank
                    skt += 1
                    if gqc == 0 and kt == 0:
                        W(p, "tensor", lc.sem, lc_all)
                    if s_bankfree[sbank]:
                        W(p, "tensor", se, s_bankfree[sbank])
                    diag = kt >= 4 * qc
                    j = kt - 4 * qc
                    def mmS(t, pp=pp, qc=qc, kt=kt, sbank=sbank, diag=diag):
                        ins = t.matmul(
                            ps[sbank][:], kb[pp][:, kt * 128:(kt + 1) * 128],
                            qb[pp][:, qc * 512:(qc + 1) * 512],
                            start=True, stop=not diag, skip_group_check=True)
                        if not diag:
                            ins.then_inc(te, 1)
                    p.add("tensor", mmS)
                    if diag:
                        p.add("tensor", lambda t, sbank=sbank, j=j: t.matmul(
                            ps[sbank][:], idb[:], mkb[:, j * 512:(j + 1) * 512],
                            start=False, stop=True, skip_group_check=True).then_inc(te, 1))
                    events[("te_s", kt)] = cte.inc()

                def emit_exp(kt, pp=None, first=False):
                    sbank = events[("sbank", kt)]
                    slot = kt % 4
                    W(p, "scalar", te, events[("te_s", kt)])
                    if slot in slot_free:
                        W(p, "scalar", te, slot_free[slot])
                    if first:
                        W(p, "scalar", ve, ve_prelude)
                    p.add("scalar", lambda s, sbank=sbank, slot=slot, pp=pp: s.activation(
                        at[:, slot * 512:(slot + 1) * 512], ps[sbank][:], AF.Exp,
                        bias=negm[:, pp:pp + 1]).then_inc(se, 1))
                    ev = cse.inc()
                    events[("se_exp", kt)] = ev
                    s_bankfree[sbank] = ev

                def emit_AVZ(kt, pp=None, nkt=None, pavb=None, pzb=None, events=None, g=None):
                    slot = kt % 4
                    W(p, "tensor", se, events[("se_exp", kt)])
                    if kt == 0:
                        for bname in (("pav", g % 2), ("pz", g % 2)):
                            if bname in bankfree:
                                sem, val = bankfree[bname]
                                W(p, "tensor", sem, val)
                    p.add("tensor", lambda t, pp=pp, kt=kt, slot=slot, nkt=nkt, pavb=pavb: t.matmul(
                        pavb[:], vb[pp][:, kt * 128:(kt + 1) * 128],
                        at[:, slot * 512:(slot + 1) * 512],
                        start=(kt == 0), stop=(kt == nkt - 1), skip_group_check=True))
                    p.add("tensor", lambda t, kt=kt, slot=slot, nkt=nkt, pzb=pzb: t.matmul(
                        pzb[:], ones_bf[:], at[:, slot * 512:(slot + 1) * 512],
                        start=(kt == 0), stop=(kt == nkt - 1),
                        skip_group_check=True).then_inc(te, 1))
                    ev = cte.inc()
                    events[("te_avz", kt)] = ev
                    slot_free[slot] = ev

                def make_tail(pp, qc, nkt, pavb, pzb, events, g):
                    def tail():
                        emit_AVZ(nkt - 2, pp=pp, nkt=nkt, pavb=pavb, pzb=pzb, events=events, g=g)
                        emit_AVZ(nkt - 1, pp=pp, nkt=nkt, pavb=pavb, pzb=pzb, events=events, g=g)
                        te_last = events[("te_avz", nkt - 1)]
                        W(p, "vector", te, te_last)
                        rzslot = g % 2
                        if ("rz", rzslot) in bankfree:
                            sem, val = bankfree[("rz", rzslot)]
                            W(p, "vector", sem, val)
                        p.add("vector", lambda v, pzb=pzb, rzslot=rzslot: v.reciprocal(
                            rz[rzslot][:], pzb[:]).then_inc(ve, 1))
                        ve_rz = cve.inc()
                        bankfree[("pz", g % 2)] = (ve, ve_rz)
                        pending_epi[0] = make_epi(g, pp, qc, pavb, ve_rz, rzslot)
                    return tail

                def make_epi(g, pp, qc, pavb, ve_rz, rzslot):
                    def epi():
                        W(p, "tensor", ve, ve_rz)
                        if "przb" in bankfree:
                            sem, val = bankfree["przb"]
                            W(p, "tensor", sem, val)
                        p.add("tensor", lambda t, rzslot=rzslot: t.matmul(
                            przb[:], ones_f[:], rz[rzslot][:], start=True, stop=True,
                            skip_group_check=True).then_inc(te, 1))
                        te_rzb = cte.inc()
                        bankfree[("rz", rzslot)] = (te, te_rzb)
                        W(p, "vector", te, te_rzb)
                        p.add("vector", lambda v: v.tensor_copy(rzb[:], przb[:]).then_inc(ve, 1))
                        ve_cp = cve.inc()
                        bankfree["przb"] = (ve, ve_cp)
                        W(p, "vector", ve, ve_cp)
                        if g >= 2 and (g - 2) in so_avs:
                            W(p, "vector", sa[g % 2].sem, so_avs[g - 2])
                        p.add("vector", lambda v, g=g, pavb=pavb: v.tensor_tensor(
                            avs[g % 2][:], pavb[:], rzb[:], ALU.mult).then_inc(ve, 1))
                        ve_nrm = cve.inc()
                        bankfree[("pav", g % 2)] = (ve, ve_nrm)
                        W(p, "sync", ve, ve_nrm)
                        p.add("sync", lambda sy, pp=pp, qc=qc, g=g: sy.dma_start(
                            out=avn_d[pp][0:128, qc * 512:(qc + 1) * 512],
                            in_=avs[g % 2][:]).then_inc(sa[g % 2].sem, 16))
                        so_avs[g] = sa[g % 2].inc()
                    return epi

                emit_S(0, pp=pp, qc=qc)
                emit_exp(0, pp=pp, first=(gqc == 0))
                emit_S(1, pp=pp, qc=qc)
                emit_exp(1, pp=pp)
                if pending_tail[0] is not None:
                    pending_tail[0]()
                    pending_tail[0] = None
                epi_flush_at = min(5, nkt - 1)
                for kt in range(2, nkt):
                    emit_S(kt, pp=pp, qc=qc)
                    emit_exp(kt, pp=pp)
                    if kt >= 4:
                        emit_AVZ(kt - 4 + 2, pp=pp, nkt=nkt, pavb=pavb, pzb=pzb,
                                 events=events, g=gqc) if False else None
                    if kt == epi_flush_at and pending_epi[0] is not None:
                        pending_epi[0]()
                        pending_epi[0] = None
                    emit_AVZ(kt - 2, pp=pp, nkt=nkt, pavb=pavb, pzb=pzb,
                             events=events, g=gqc)
                if pending_epi[0] is not None:
                    pending_epi[0]()
                    pending_epi[0] = None
                pending_tail[0] = make_tail(pp, qc, nkt, pavb, pzb, events, gqc)
                gqc += 1
        if pending_tail[0] is not None:
            pending_tail[0]()
            pending_tail[0] = None
        if pending_epi[0] is not None:
            pending_epi[0]()
            pending_epi[0] = None
        for ss in (sa[0], sa[1]):
            W(p, "sync", ss.sem, ss.total())
        p.run(nc)
    return nc



def build_mid(layer):
    IC = 64 if layer == 0 else 44
    nc = bass.Bass("TRN2", target_bir_lowering=False, debug=False)
    x_sl = nc.dram_tensor("x_sl", [T, D], F32, kind="ExternalInput").ap()
    av_d = nc.dram_tensor("av_d", [16, 128, T], BF16, kind="ExternalInput").ap()
    wo_d = nc.dram_tensor("wo_d", [4, 4, 128, 4 * 512], F32, kind="ExternalInput").ap()
    wgu_d = nc.dram_tensor("wgu_d", [IC, 128, 4096], I32, kind="ExternalInput").ap()
    wd_d = nc.dram_tensor("wd_d", [2, IC // 2, 128, 2048], I32, kind="ExternalInput").ap()
    sg_d = nc.dram_tensor("sg_d", [128, IC], F32, kind="ExternalInput").ap()
    su_d = nc.dram_tensor("su_d", [128, IC], F32, kind="ExternalInput").ap()
    sd_d = nc.dram_tensor("sd_d", [1, D], F32, kind="ExternalInput").ap()
    idf_d = nc.dram_tensor("idf_d", [128, 128], F32, kind="ExternalInput").ap()
    if layer == 0:
        wa1_d = nc.dram_tensor("wa1_d", [48, 128, 2048], F32, kind="ExternalInput").ap()
        bq_d = nc.dram_tensor("bq_d", [128, 16], F32, kind="ExternalInput").ap()
        bk_d = nc.dram_tensor("bk_d", [128, 16], F32, kind="ExternalInput").ap()
        bv_d = nc.dram_tensor("bv_d", [128, 16], F32, kind="ExternalInput").ap()
        lng_d = nc.dram_tensor("lng_d", [128, 16], F32, kind="ExternalInput").ap()
        lnb_d = nc.dram_tensor("lnb_d", [128, 16], F32, kind="ExternalInput").ap()
        q1_o = nc.dram_tensor("q1_o", [D, T], BF16, kind="ExternalOutput").ap()
        k1_o = nc.dram_tensor("k1_o", [D, T], BF16, kind="ExternalOutput").ap()
        v1_o = nc.dram_tensor("v1_o", [D, T], BF16, kind="ExternalOutput").ap()
    else:
        bo_d = nc.dram_tensor("bo_d", [1, D], F32, kind="ExternalInput").ap()
    x_out = nc.dram_tensor("x_out", [T, D], F32, kind="ExternalOutput").ap()
    x1_scr = nc.dram_tensor("x1_scr", [T, D], F32, kind="Internal").ap()

    es = contextlib.ExitStack()
    with es:
        A0, B0, C0 = 0, 16384, 32768
        BBW = C0 + IC * 512
        bb = es.enter_context(nc.sbuf_tensor("bb", [128, BBW], BF16))
        xt0 = es.enter_context(nc.sbuf_tensor("xt0", [128, D], F32))
        xt = [xt0, xt0]
        z32 = es.enter_context(nc.sbuf_tensor("z32", [128, D], F32))
        z32b = es.enter_context(nc.sbuf_tensor("z32b", [128, D], F32))
        zb2 = [z32, z32b]
        wob = es.enter_context(nc.sbuf_tensor("wob", [128, 2 * 2048], BF16))
        wgub = [es.enter_context(nc.sbuf_tensor(f"wgub{i}", [128, 4096], BF16)) for i in range(2)]
        wdb = [es.enter_context(nc.sbuf_tensor(f"wdb{i}", [128, 2048], BF16)) for i in range(2)]
        sgb = es.enter_context(nc.sbuf_tensor("sgb", [128, IC], F32))
        sub = es.enter_context(nc.sbuf_tensor("sub", [128, IC], F32))
        sdr = es.enter_context(nc.sbuf_tensor("sdr", [1, D], F32))
        sdb = es.enter_context(nc.sbuf_tensor("sdb", [128, D], BF16))
        idf = es.enter_context(nc.sbuf_tensor("idf", [128, 128], F32))
        ones_bf = es.enter_context(nc.sbuf_tensor("ones_bf", [128, 1], BF16))
        ones_f = es.enter_context(nc.sbuf_tensor("ones_f", [1, 128], F32))
        ones5f = es.enter_context(nc.sbuf_tensor("ones5f", [1, 512], F32))
        cneg = es.enter_context(nc.sbuf_tensor("cneg", [1, 128], BF16))
        ceps = es.enter_context(nc.sbuf_tensor("ceps", [128, 1], F32))
        stat = es.enter_context(nc.sbuf_tensor("stat", [128, 36], F32))
        xs_bf = es.enter_context(nc.sbuf_tensor("xs_bf", [1, 512], BF16))
        hs_f = es.enter_context(nc.sbuf_tensor("hs_f", [1, 512], F32))
        sgt_t = es.enter_context(nc.sbuf_tensor("sgt_t", [128, 512], BF16))
        ut_t = es.enter_context(nc.sbuf_tensor("ut_t", [128, 512], BF16))
        sgt = sgt_t[:]
        ut = ut_t[:]
        if layer == 0:
            wab = [bb[:, A0 + i * 2048: A0 + (i + 1) * 2048] for i in range(2)]
            stg = [bb[:, A0 + 4096 + i * T: A0 + 4096 + (i + 1) * T] for i in range(2)]
            bqs = es.enter_context(nc.sbuf_tensor("bqs", [128, 16], F32))
            bkb = es.enter_context(nc.sbuf_tensor("bkb", [128, 16], F32))
            bvb = es.enter_context(nc.sbuf_tensor("bvb", [128, 16], F32))
            lng = es.enter_context(nc.sbuf_tensor("lng", [128, 16], F32))
            lnb = es.enter_context(nc.sbuf_tensor("lnb", [128, 16], F32))
        else:
            bo_bf = es.enter_context(nc.sbuf_tensor("bo_bf", [1, D], BF16))
            ones_r = es.enter_context(nc.sbuf_tensor("ones_r", [1, 128], BF16))
        pb = [es.enter_context(nc.psum_tensor(f"pb{i}", [128, 512], F32)) for i in range(8)]

        lc = SlotSem(es, nc, "lc")
        lwo = [SlotSem(es, nc, f"lwo{i}") for i in range(2)]
        lx0 = SlotSem(es, nc, "lx0")
        lx = [lx0, lx0]
        lgu = [SlotSem(es, nc, f"lgu{i}") for i in range(2)]
        lguB = [SlotSem(es, nc, f"lguB{i}") for i in range(2)]
        lwd = [SlotSem(es, nc, f"lwd{i}") for i in range(2)]
        sx1 = SlotSem(es, nc, "sx1")
        sxo = SlotSem(es, nc, "sxo")
        if layer == 0:
            lwa = [SlotSem(es, nc, f"lwa{i}") for i in range(2)]
            sst = [SlotSem(es, nc, f"sst{i}") for i in range(2)]
        ve = es.enter_context(nc.semaphore("ve"))
        se = es.enter_context(nc.semaphore("se"))
        te = es.enter_context(nc.semaphore("te"))
        sems = {"ve": ve, "se": se, "te": te, "sx1": sx1.sem, "sxo": sxo.sem,
                "lx0": lx0.sem, "lx1": lx0.sem}
        cve, cse, cte = Ctr(), Ctr(), Ctr()
        p = Plan()
        bankfree = {}

        def gwait(eng, guard):
            if guard is not None and guard[1] > 0:
                W(p, eng, sems[guard[0]], guard[1])

        def bwait(bk, eng="tensor"):
            gwait(eng, bankfree.get(bk))

        for dc in range(16):
            p.add("gpsimd", lambda g, dc=dc: g.dma_start(
                out=bb[:, C0 + dc * T: C0 + (dc + 1) * T], in_=av_d[dc]).then_inc(lc.sem, 16))
            lc.inc()
        consts = [(sgb, sg_d), (sub, su_d), (sdr, sd_d), (idf, idf_d)]
        if layer == 0:
            consts += [(bkb, bk_d), (bvb, bv_d), (lng, lng_d), (lnb, lnb_d), (bqs, bq_d)]
        else:
            consts += [(bo_bf, bo_d)]
        for t_sb, t_dr in consts:
            p.add("gpsimd", lambda g, a=t_sb, b=t_dr: g.dma_start(out=a[:], in_=b).then_inc(lc.sem, 16))
            lc.inc()
        lc_all = lc.total()
        for buf, val in [(ones_bf, 1.0), (ones_f, 1.0), (ones5f, 1.0), (cneg, -128.0), (ceps, EPS)]:
            p.add("vector", lambda v, b=buf, val=val: v.memset(b[:], val).then_inc(ve, 1))
            cve.inc()
        if layer == 0:
            W(p, "vector", lc.sem, lc_all)
            p.add("vector", lambda v: v.tensor_scalar(
                bqs[:], bqs[:], ISQ, None, ALU.mult).then_inc(ve, 1))
            cve.inc()
        else:
            p.add("vector", lambda v: v.memset(ones_r[:], 1.0).then_inc(ve, 1))
            cve.inc()
        ve_consts = cve.v
        W(p, "tensor", lc.sem, lc_all)
        W(p, "tensor", ve, ve_consts)
        for og in range(4):
            p.add("tensor", lambda t, og=og: t.matmul(
                pb[og][:], ones_f[:], sdr[:, og * 512:(og + 1) * 512], start=True,
                stop=True, skip_group_check=True).then_inc(te, 1))
            te_b = cte.inc()
            W(p, "vector", te, te_b)
            p.add("vector", lambda v, og=og: v.tensor_copy(
                sdb[:, og * 512:(og + 1) * 512], pb[og][:]).then_inc(ve, 1))
            bankfree[og] = ("ve", cve.inc())

        te_grp = {}
        ve_y0 = {}
        for og in range(4):
            for dcg in range(4):
                gi = og * 4 + dcg
                wsl = gi % 2
                if gi >= 2:
                    W(p, "gpsimd", te, te_grp[gi - 2])
                p.add("gpsimd", lambda g, og=og, dcg=dcg, wsl=wsl: g.dma_start(
                    out=wob[:, wsl * 2048:(wsl + 1) * 2048], in_=wo_d[og, dcg]).then_inc(lwo[wsl].sem, 16))
                ld_wo = lwo[wsl].inc()
                W(p, "tensor", lwo[wsl].sem, ld_wo)
                if dcg == 0:
                    for tt in range(8):
                        bwait(tt)
                if dcg < 3:
                    for tt in range(8):
                        for dd in range(4):
                            dc = dcg * 4 + dd
                            def mwo(t, tt=tt, dd=dd, dc=dc, wsl=wsl, last=(tt == 7 and dd == 3)):
                                ins = t.matmul(
                                    pb[tt][:], bb[:, C0 + dc * T + tt * 128: C0 + dc * T + (tt + 1) * 128],
                                    wob[:, wsl * 2048 + dd * 512: wsl * 2048 + (dd + 1) * 512],
                                    start=(dc == 0), stop=False,
                                    skip_group_check=True)
                                if last:
                                    ins.then_inc(te, 1)
                            p.add("tensor", mwo)
                    te_grp[gi] = cte.inc()
                else:
                    for tt in range(8):
                        for dd in range(4):
                            dc = dcg * 4 + dd
                            def mwo2(t, tt=tt, dd=dd, dc=dc, wsl=wsl):
                                ins = t.matmul(
                                    pb[tt][:], bb[:, C0 + dc * T + tt * 128: C0 + dc * T + (tt + 1) * 128],
                                    wob[:, wsl * 2048 + dd * 512: wsl * 2048 + (dd + 1) * 512],
                                    start=False, stop=(dd == 3 and layer == 0),
                                    skip_group_check=True)
                                if dd == 3 and layer == 0:
                                    ins.then_inc(te, 1)
                            p.add("tensor", mwo2)
                        if layer == 1:
                            def mbo(t, og=og, tt=tt):
                                t.matmul(
                                    pb[tt][:], ones_r[:], bo_bf[:, og * 512:(og + 1) * 512],
                                    start=False, stop=True,
                                    skip_group_check=True).then_inc(te, 1)
                            p.add("tensor", mbo)
                        te_tt = cte.inc()
                        W(p, "vector", te, te_tt)
                        p.add("vector", lambda v, og=og, tt=tt: v.tensor_copy(
                            bb[:, A0 + tt * 2048 + og * 512: A0 + tt * 2048 + (og + 1) * 512],
                            pb[tt][:]).then_inc(ve, 1))
                        ve_y0[(og, tt)] = cve.inc()
                        bankfree[tt] = ("ve", ve_y0[(og, tt)])
                    te_grp[gi] = cte.v


        def emit_stats(src_fn, c0, fw_v=(), fw_s=()):
            for w in fw_v:
                gwait("vector", w)
            p.add("vector", lambda v, c0=c0: v.tensor_reduce(
                stat[:, c0:c0 + 1], src_fn(), AX.X, ALU.add).then_inc(ve, 1))
            e_red = cve.inc()
            for w in fw_s:
                gwait("scalar", w)
            p.add("scalar", lambda s, c0=c0: s.activation(
                bb[:, C0:C0 + D], src_fn(), AF.Square,
                accum_out=stat[:, c0 + 2:c0 + 3]).then_inc(se, 1))
            e_sq = cse.inc()
            W(p, "vector", ve, e_red)
            W(p, "vector", se, e_sq)
            p.add("vector", lambda v, c0=c0: v.tensor_scalar(
                stat[:, c0 + 4:c0 + 5], stat[:, c0:c0 + 1], 1.0 / D, None,
                ALU.mult).then_inc(ve, 1))
            e_m = cve.inc()
            W(p, "vector", ve, e_m)
            p.add("vector", lambda v, c0=c0: v.tensor_tensor(
                stat[:, c0 + 6:c0 + 7], stat[:, c0 + 4:c0 + 5], stat[:, c0 + 4:c0 + 5],
                ALU.mult).then_inc(ve, 1))
            e_t = cve.inc()
            W(p, "vector", ve, e_t)
            p.add("vector", lambda v, c0=c0: v.tensor_scalar(
                stat[:, c0 + 8:c0 + 9], stat[:, c0 + 2:c0 + 3], 1.0 / D,
                stat[:, c0 + 6:c0 + 7], ALU.mult, ALU.subtract).then_inc(ve, 1))
            e_var = cve.inc()
            W(p, "scalar", ve, e_var)
            p.add("scalar", lambda s, c0=c0: s.activation(
                stat[:, c0 + 6:c0 + 7], stat[:, c0 + 8:c0 + 9], AF.Ln,
                bias=ceps[:]).then_inc(se, 1))
            e_ln = cse.inc()
            W(p, "scalar", se, e_ln)
            p.add("scalar", lambda s, c0=c0: s.activation(
                stat[:, c0 + 10:c0 + 11], stat[:, c0 + 6:c0 + 7], AF.Exp,
                scale=-0.5).then_inc(se, 1))
            e_rs = cse.inc()
            W(p, "vector", se, e_rs)
            p.add("vector", lambda v, c0=c0: v.tensor_scalar(
                stat[:, c0 + 12:c0 + 13], stat[:, c0 + 4:c0 + 5],
                stat[:, c0 + 10:c0 + 11], -1.0, ALU.mult, ALU.mult).then_inc(ve, 1))
            e_mrs = cve.inc()
            return e_rs, e_mrs, e_var

        so_x1 = {}
        ve_xn1 = {}
        xt_guard = {0: None}
        tile_guard = {0: None, 1: None}
        tstate = {}

        def p2_stageA(tt):
            sl = tt % 2
            st_ = {}
            gwait("gpsimd", xt_guard[0])
            p.add("gpsimd", lambda g, tt=tt: g.dma_start(
                out=xt0[:], in_=x_sl[tt * 128:(tt + 1) * 128, :]).then_inc(lx0.sem, 16))
            st_["ld_x"] = ("lx0", lx0.inc())
            def y0sl(tt=tt):
                return bb[:, A0 + tt * 2048: A0 + (tt + 1) * 2048]
            fw = [("ve", ve_y0[(3, tt)]), tile_guard[sl]]
            fw = [w for w in fw if w is not None]
            _, st_["eA_mrs"], _ = emit_stats(y0sl, sl, fw_v=fw, fw_s=fw)
            tstate[tt] = st_

        def p2_stageB(tt):
            sl = tt % 2
            zb = zb2[sl]
            st_ = tstate[tt]
            W(p, "scalar", ve, st_["eA_mrs"])
            p.add("scalar", lambda s, tt=tt, sl=sl, zb=zb: s.activation(
                zb[:], bb[:, A0 + tt * 2048: A0 + (tt + 1) * 2048], AF.Identity,
                scale=stat[:, sl + 10:sl + 11], bias=stat[:, sl + 12:sl + 13]).then_inc(se, 1))
            eE = cse.inc()
            W(p, "vector", se, eE)
            gwait("vector", st_["ld_x"])
            p.add("vector", lambda v, zb=zb: v.tensor_tensor(
                zb[:], zb[:], xt0[:], ALU.add).then_inc(ve, 1))
            eF = cve.inc()
            xt_guard[0] = ("ve", eF)
            W(p, "sync", ve, eF)
            p.add("sync", lambda sy, tt=tt, zb=zb: sy.dma_start(
                out=x1_scr[tt * 128:(tt + 1) * 128, :], in_=zb[:]).then_inc(sx1.sem, 16))
            so_x1[tt] = sx1.inc()
            _, st_["eB_mrs"], _ = emit_stats(lambda zb=zb: zb[:], 14 + sl,
                                             fw_v=[("ve", eF)], fw_s=[("ve", eF)])

        def p2_stageC(tt):
            sl = tt % 2
            zb = zb2[sl]
            st_ = tstate[tt]
            W(p, "scalar", sx1.sem, so_x1[tt])
            W(p, "scalar", ve, st_["eB_mrs"])
            p.add("scalar", lambda s, sl=sl, zb=zb: s.activation(
                zb[:], zb[:], AF.Identity, scale=stat[:, sl + 24:sl + 25],
                bias=stat[:, sl + 26:sl + 27]).then_inc(se, 1))
            eJ = cse.inc()
            W(p, "tensor", se, eJ)
            te_tr_last = 0
            for dc in range(16):
                bk = dc % 2
                bwait(bk)
                p.add("tensor", lambda t, dc=dc, bk=bk, zb=zb: t.transpose(
                    pb[bk][:, 0:128], zb[:, dc * 128:(dc + 1) * 128],
                    idf[:]).then_inc(te, 1))
                te_tr = cte.inc()
                te_tr_last = te_tr
                W(p, "vector", te, te_tr)
                p.add("vector", lambda v, tt=tt, dc=dc, bk=bk: v.tensor_copy(
                    bb[:, B0 + dc * T + tt * 128: B0 + dc * T + (tt + 1) * 128],
                    pb[bk][:, 0:128]).then_inc(ve, 1))
                ve_e = cve.inc()
                ve_xn1[(tt, dc)] = ve_e
                bankfree[bk] = ("ve", ve_e)
            tile_guard[sl] = ("te", te_tr_last)
            del tstate[tt]

        for step in range(10):
            if 2 <= step:
                p2_stageC(step - 2)
            if 1 <= step <= 8:
                p2_stageB(step - 1)
            if step < 8:
                p2_stageA(step)
        ve_xn1_all = ve_xn1[(7, 15)]

        so_x2 = {}
        xs_guard = None
        hs_guard = None
        y1_scalar_guard = None
        gu_slot_guard = {0: None, 1: None}
        gu_gate_guard = {0: None, 1: None}
        wd_slot_guard = {0: None, 1: None}
        for tg in range(2):
            bwait(6)
            W(p, "tensor", ve, ve_xn1[(3 if tg == 0 else 7, 15)])
            for hc in range(16):
                def mxs(t, hc=hc, tg=tg):
                    ins = t.matmul(
                        pb[6][0:1, :], ones_bf[:],
                        bb[:, B0 + hc * T + tg * 512: B0 + hc * T + tg * 512 + 512],
                        start=(hc == 0), stop=(hc == 15), skip_group_check=True)
                    if hc == 15:
                        ins.then_inc(te, 1)
                p.add("tensor", mxs)
            te_xs = cte.inc()
            W(p, "vector", te, te_xs)
            gwait("vector", xs_guard)
            p.add("vector", lambda v: v.tensor_copy(xs_bf[:], pb[6][0:1, :]).then_inc(ve, 1))
            ve_xs = cve.inc()
            bankfree[6] = ("ve", ve_xs)

            te_gu = {}
            se_sil = {}
            ve_ue = {}
            ve_h = {}
            for ic in range(IC):
                wsl = ic % 2
                gwait("gpsimd", gu_gate_guard[wsl])
                p.add("gpsimd", lambda g, ic=ic, wsl=wsl: g.dma_start(
                    out=wgub[wsl][:, 0:2048], in_=wgu_d[ic, :, 0:2048]).then_inc(lgu[wsl].sem, 16))
                ld_wg2 = lgu[wsl].inc()
                gwait("gpsimd", gu_slot_guard[wsl])
                p.add("gpsimd", lambda g, ic=ic, wsl=wsl: g.dma_start(
                    out=wgub[wsl][:, 2048:4096], in_=wgu_d[ic, :, 2048:4096]).then_inc(lguB[wsl].sem, 16))
                ld_wu = lguB[wsl].inc()
                W(p, "tensor", lgu[wsl].sem, ld_wg2)
                W(p, "tensor", ve, ve_xs)
                bwait(4)
                bwait(5)
                for hc in range(16):
                    p.add("tensor", lambda t, ic=ic, hc=hc, tg=tg, wsl=wsl: t.matmul(
                        pb[4][:], wgub[wsl][:, hc * 128:(hc + 1) * 128],
                        bb[:, B0 + hc * T + tg * 512: B0 + hc * T + tg * 512 + 512],
                        start=(hc == 0), stop=False, skip_group_check=True))
                p.add("tensor", lambda t: t.matmul(
                    pb[4][:], cneg[:], xs_bf[:], start=False, stop=True,
                    skip_group_check=True).then_inc(te, 1))
                te_g = cte.inc()
                W(p, "tensor", lguB[wsl].sem, ld_wu)
                for hc in range(16):
                    p.add("tensor", lambda t, ic=ic, hc=hc, tg=tg, wsl=wsl: t.matmul(
                        pb[5][:], wgub[wsl][:, 2048 + hc * 128: 2048 + (hc + 1) * 128],
                        bb[:, B0 + hc * T + tg * 512: B0 + hc * T + tg * 512 + 512],
                        start=(hc == 0), stop=False, skip_group_check=True))
                p.add("tensor", lambda t: t.matmul(
                    pb[5][:], cneg[:], xs_bf[:], start=False, stop=True,
                    skip_group_check=True).then_inc(te, 1))
                te_gu[ic] = cte.inc()
                gu_gate_guard[wsl] = ("te", te_g)
                gu_slot_guard[wsl] = ("te", te_gu[ic])
                xs_guard = ("te", te_gu[ic])
                W(p, "scalar", te, te_g)
                if ic >= 1:
                    W(p, "scalar", ve, ve_h[ic - 1])
                elif tg == 1:
                    W(p, "scalar", ve, ve_h_prevtg)
                p.add("scalar", lambda s, ic=ic: s.activation(
                    sgt, pb[4][:], AF.Silu, scale=sgb[:, ic:ic + 1]).then_inc(se, 1))
                se_sil[ic] = cse.inc()
                bankfree[4] = ("se", se_sil[ic])
                W(p, "vector", te, te_gu[ic])
                if ic >= 1:
                    W(p, "vector", ve, ve_h[ic - 1])
                elif tg == 1:
                    W(p, "vector", ve, ve_h_prevtg)
                p.add("vector", lambda v, ic=ic: v.tensor_scalar(
                    ut, pb[5][:], sub[:, ic:ic + 1], None, ALU.mult).then_inc(ve, 1))
                ve_ue[ic] = cve.inc()
                bankfree[5] = ("ve", ve_ue[ic])
                W(p, "vector", se, se_sil[ic])
                W(p, "vector", ve, ve_ue[ic])
                p.add("vector", lambda v, ic=ic: v.tensor_tensor(
                    bb[:, C0 + ic * 512: C0 + (ic + 1) * 512], sgt, ut,
                    ALU.mult).then_inc(ve, 1))
                ve_h[ic] = cve.inc()
            ve_h_all = ve_h[IC - 1]
            ve_h_prevtg = ve_h_all

            bwait(7)
            W(p, "tensor", ve, ve_h_all)
            for ic in range(IC):
                def mhs(t, ic=ic):
                    ins = t.matmul(
                        pb[7][0:1, :], ones_bf[:], bb[:, C0 + ic * 512: C0 + (ic + 1) * 512],
                        start=(ic == 0), stop=(ic == IC - 1), skip_group_check=True)
                    if ic == IC - 1:
                        ins.then_inc(te, 1)
                p.add("tensor", mhs)
            te_hs = cte.inc()
            W(p, "vector", te, te_hs)
            gwait("vector", hs_guard)
            p.add("vector", lambda v: v.tensor_copy(hs_f[:], pb[7][0:1, :]).then_inc(ve, 1))
            ve_hs = cve.inc()
            W(p, "vector", ve, ve_hs)
            p.add("vector", lambda v: v.tensor_scalar(
                hs_f[:], hs_f[:], -128.0, None, ALU.mult).then_inc(ve, 1))
            ve_hsn = cve.inc()
            bankfree[7] = ("ve", ve_hsn)

            te_wd = {}
            ve_y1 = {}
            te_corr_done = 0
            for ogp in range(2):
                for ic in range(IC):
                    wsl = (ic // 2) % 2
                    if ic % 2 == 0:
                        gwait("gpsimd", wd_slot_guard[wsl])
                        p.add("gpsimd", lambda g, ogp=ogp, ic=ic, wsl=wsl: g.dma_start(
                            out=wdb[wsl][:], in_=wd_d[ogp, ic // 2]).then_inc(lwd[wsl].sem, 16))
                        ld_wd = lwd[wsl].inc()
                    W(p, "tensor", lwd[wsl].sem, ld_wd)
                    if ic == 0:
                        W(p, "tensor", ve, ve_hsn)
                        for bk in range(8):
                            bwait(bk)
                    for ts_ in range(4):
                        for o2 in range(2):
                            bk = ts_ * 2 + o2
                            def mdn(t, ic=ic, ts_=ts_, o2=o2, bk=bk, wsl=wsl,
                                    last=(ts_ == 3 and o2 == 1)):
                                ins = t.matmul(
                                    pb[bk][:],
                                    bb[:, C0 + ic * 512 + ts_ * 128: C0 + ic * 512 + (ts_ + 1) * 128],
                                    wdb[wsl][:, (ic % 2) * 1024 + o2 * 512: (ic % 2) * 1024 + (o2 + 1) * 512],
                                    start=(ic == 0), stop=False, skip_group_check=True)
                                if last:
                                    ins.then_inc(te, 1)
                            p.add("tensor", mdn)
                    te_wd[(ogp, ic)] = cte.inc()
                    wd_slot_guard[wsl] = ("te", te_wd[(ogp, ic)])
                for ts_ in range(4):
                    for o2 in range(2):
                        bk = ts_ * 2 + o2
                        og = ogp * 2 + o2
                        def mcor(t, ts_=ts_, bk=bk):
                            t.matmul(
                                pb[bk][:], hs_f[:, ts_ * 128:(ts_ + 1) * 128],
                                ones5f[:], start=False, stop=True,
                                skip_group_check=True).then_inc(te, 1)
                        p.add("tensor", mcor)
                        te_corr = cte.inc()
                        W(p, "vector", te, te_corr)
                        gwait("vector", y1_scalar_guard)
                        p.add("vector", lambda v, ts_=ts_, og=og, bk=bk: v.tensor_tensor(
                            bb[:, A0 + ts_ * 2048 + og * 512: A0 + ts_ * 2048 + (og + 1) * 512],
                            pb[bk][:], sdb[:, og * 512:(og + 1) * 512], ALU.mult).then_inc(ve, 1))
                        ve_y1[(ts_, og)] = cve.inc()
                        bankfree[bk] = ("ve", ve_y1[(ts_, og)])
                hs_guard = ("te", cte.v)

            p4state = {}

            def p4_stageA(ts_, tg=tg):
                tt = tg * 4 + ts_
                sl = tt % 2
                st_ = {}
                if ts_ == 0:
                    W(p, "gpsimd", sx1.sem, sx1.total())
                gwait("gpsimd", xt_guard[0])
                p.add("gpsimd", lambda g, tt=tt: g.dma_start(
                    out=xt0[:], in_=x1_scr[tt * 128:(tt + 1) * 128, :]).then_inc(lx0.sem, 16))
                st_["ld_x"] = ("lx0", lx0.inc())
                def y1sl(ts_=ts_):
                    return bb[:, A0 + ts_ * 2048: A0 + (ts_ + 1) * 2048]
                fw = [("ve", ve_y1[(ts_, og)]) for og in range(4)]
                if tile_guard[sl] is not None:
                    fw = fw + [tile_guard[sl]]
                _, st_["eA_mrs"], _ = emit_stats(y1sl, sl, fw_v=fw, fw_s=fw)
                p4state[ts_] = st_

            def p4_stageB(ts_, tg=tg):
                tt = tg * 4 + ts_
                sl = tt % 2
                zb = zb2[sl]
                st_ = p4state[ts_]
                W(p, "scalar", ve, st_["eA_mrs"])
                p.add("scalar", lambda s, ts_=ts_, sl=sl, zb=zb: s.activation(
                    zb[:], bb[:, A0 + ts_ * 2048: A0 + (ts_ + 1) * 2048], AF.Identity,
                    scale=stat[:, sl + 10:sl + 11], bias=stat[:, sl + 12:sl + 13]).then_inc(se, 1))
                eE = cse.inc()
                W(p, "vector", se, eE)
                gwait("vector", st_["ld_x"])
                p.add("vector", lambda v, zb=zb: v.tensor_tensor(
                    zb[:], zb[:], xt0[:], ALU.add).then_inc(ve, 1))
                ve_x2 = cve.inc()
                xt_guard[0] = ("ve", ve_x2)
                W(p, "sync", ve, ve_x2)
                p.add("sync", lambda sy, tt=tt, zb=zb: sy.dma_start(
                    out=x_out[tt * 128:(tt + 1) * 128, :], in_=zb[:]).then_inc(sxo.sem, 16))
                so_x2[tt] = sxo.inc()
                tile_guard[sl] = ("sxo", so_x2[tt])
                return eE

            eE_last = 0
            for step in range(5):
                if 1 <= step:
                    eE_last = p4_stageB(step - 1)
                if step < 4:
                    p4_stageA(step)
            y1_scalar_guard = ("se", eE_last)

        if layer == 1:
            W(p, "sync", sxo.sem, sxo.total())
            p.run(nc)
            return nc

        ve_xn2 = {}
        bstate = {}

        def p4b_stageA(tt):
            sl = tt % 2
            st_ = {}
            if tt == 0:
                W(p, "gpsimd", sxo.sem, sxo.total())
            gwait("gpsimd", xt_guard[0])
            p.add("gpsimd", lambda g, tt=tt: g.dma_start(
                out=xt0[:], in_=x_out[tt * 128:(tt + 1) * 128, :]).then_inc(lx0.sem, 16))
            ld_x2 = ("lx0", lx0.inc())
            fw = [ld_x2]
            if tile_guard[sl] is not None:
                fw = fw + [tile_guard[sl]]
            eA_rs, _, _ = emit_stats(lambda: xt0[:], sl, fw_v=fw, fw_s=fw)
            W(p, "vector", se, eA_rs)
            p.add("vector", lambda v, sl=sl: v.tensor_scalar(
                stat[:, 28 + sl:29 + sl], stat[:, 8 + sl:9 + sl],
                stat[:, 10 + sl:11 + sl], stat[:, 10 + sl:11 + sl], ALU.mult,
                ALU.mult).then_inc(ve, 1))
            ve_w = cve.inc()
            W(p, "scalar", ve, ve_w)
            p.add("scalar", lambda s, sl=sl: s.activation(
                stat[:, 30 + sl:31 + sl], stat[:, 28 + sl:29 + sl], AF.Ln,
                bias=ceps[:]).then_inc(se, 1))
            se_l2 = cse.inc()
            W(p, "scalar", se, se_l2)
            p.add("scalar", lambda s, sl=sl: s.activation(
                stat[:, 30 + sl:31 + sl], stat[:, 30 + sl:31 + sl], AF.Exp,
                scale=-0.5).then_inc(se, 1))
            se_rb = cse.inc()
            W(p, "vector", se, se_rb)
            p.add("vector", lambda v, sl=sl: v.tensor_tensor(
                stat[:, 32 + sl:33 + sl], stat[:, 10 + sl:11 + sl],
                stat[:, 30 + sl:31 + sl], ALU.mult).then_inc(ve, 1))
            ve_r12 = cve.inc()
            W(p, "vector", ve, ve_r12)
            p.add("vector", lambda v, sl=sl: v.tensor_scalar(
                stat[:, 34 + sl:35 + sl], stat[:, 4 + sl:5 + sl],
                stat[:, 32 + sl:33 + sl], -1.0, ALU.mult, ALU.mult).then_inc(ve, 1))
            st_["ve_mr12"] = cve.inc()
            bstate[tt] = st_

        def p4b_stageB(tt):
            sl = tt % 2
            zb = zb2[sl]
            st_ = bstate[tt]
            W(p, "scalar", ve, st_["ve_mr12"])
            p.add("scalar", lambda s, sl=sl, zb=zb: s.activation(
                zb[:], xt0[:], AF.Identity, scale=stat[:, 32 + sl:33 + sl],
                bias=stat[:, 34 + sl:35 + sl]).then_inc(se, 1))
            eEp = cse.inc()
            xt_guard[0] = ("se", eEp)
            W(p, "tensor", se, eEp)
            te_tr_last = 0
            for dc in range(16):
                bk = dc % 2
                bwait(bk)
                p.add("tensor", lambda t, dc=dc, bk=bk, zb=zb: t.transpose(
                    pb[bk][:, 0:128], zb[:, dc * 128:(dc + 1) * 128],
                    idf[:]).then_inc(te, 1))
                te_tr = cte.inc()
                te_tr_last = te_tr
                W(p, "vector", te, te_tr)
                p.add("vector", lambda v, tt=tt, dc=dc, bk=bk: v.tensor_scalar(
                    bb[:, B0 + dc * T + tt * 128: B0 + dc * T + (tt + 1) * 128],
                    pb[bk][:, 0:128], lng[:, dc:dc + 1], lnb[:, dc:dc + 1],
                    ALU.mult, ALU.add).then_inc(ve, 1))
                ve_e = cve.inc()
                ve_xn2[(tt, dc)] = ve_e
                bankfree[bk] = ("ve", ve_e)
            tile_guard[sl] = ("te", te_tr_last)
            del bstate[tt]

        for step in range(9):
            if 1 <= step:
                p4b_stageB(step - 1)
            if step < 8:
                p4b_stageA(step)
        ve_xn2_all = ve_xn2[(7, 15)]

        te_qk = {}
        ve_evac = {}
        so_qk = {}
        for oc in range(48):
            wsl = oc % 2
            if oc >= 2:
                W(p, "gpsimd", te, te_qk[oc - 2])
            p.add("gpsimd", lambda g, oc=oc, wsl=wsl: g.dma_start(
                out=wab[wsl], in_=wa1_d[oc]).then_inc(lwa[wsl].sem, 16))
            ld_w = lwa[wsl].inc()
            for tg in range(2):
                bank = (2 * oc + tg) % 4
                W(p, "tensor", lwa[wsl].sem, ld_w)
                if oc == 0:
                    W(p, "tensor", ve, ve_xn2[(3 if tg == 0 else 7, 15)])
                bwait(bank)
                for ic in range(16):
                    def mq(t, oc=oc, tg=tg, ic=ic, bank=bank, wsl=wsl):
                        ins = t.matmul(
                            pb[bank][:], bb[:, A0 + wsl * 2048 + ic * 128: A0 + wsl * 2048 + (ic + 1) * 128],
                            bb[:, B0 + ic * T + tg * 512: B0 + ic * T + tg * 512 + 512],
                            start=(ic == 0), stop=(ic == 15), skip_group_check=True)
                        if ic == 15:
                            ins.then_inc(te, 1)
                    p.add("tensor", mq)
                te_mm = cte.inc()
                if tg == 1:
                    te_qk[oc] = te_mm
                W(p, "vector", te, te_mm)
                if oc >= 2 and tg == 0:
                    W(p, "vector", sst[wsl].sem, so_qk[oc - 2])
                kind = oc // 16
                off = oc % 16
                if kind == 0:
                    p.add("vector", lambda v, oc=oc, tg=tg, bank=bank, off=off: v.tensor_scalar(
                        bb[:, A0 + 4096 + (oc % 2) * T + tg * 512: A0 + 4096 + (oc % 2) * T + tg * 512 + 512],
                        pb[bank][:], ISQ,
                        bqs[:, off:off + 1], ALU.mult, ALU.add).then_inc(ve, 1))
                else:
                    bias = bkb if kind == 1 else bvb
                    p.add("vector", lambda v, oc=oc, tg=tg, bank=bank, off=off, bias=bias: v.tensor_scalar(
                        bb[:, A0 + 4096 + (oc % 2) * T + tg * 512: A0 + 4096 + (oc % 2) * T + tg * 512 + 512],
                        pb[bank][:],
                        bias[:, off:off + 1], None, ALU.add).then_inc(ve, 1))
                ve_evac[(oc, tg)] = cve.inc()
                bankfree[bank] = ("ve", ve_evac[(oc, tg)])
            W(p, "sync", ve, ve_evac[(oc, 1)])
            dst = (q1_o, k1_o, v1_o)[oc // 16]
            r0 = (oc % 16) * 128
            p.add("sync", lambda sy, oc=oc, dst=dst, r0=r0, wsl=wsl: sy.dma_start(
                out=dst[r0:r0 + 128, :], in_=stg[oc % 2]).then_inc(sst[wsl].sem, 16))
            so_qk[oc] = sst[wsl].inc()

        for ss in (sst[0], sst[1], sxo):
            W(p, "sync", ss.sem, ss.total())
        p.run(nc)
    return nc


MASKS_BF = None


def _masks_bf():
    global MASKS_BF
    if MASKS_BF is None:
        m = np.zeros((128, 4 * 512), np.float32)
        ki = np.arange(128)[:, None]
        qi = np.arange(512)[None, :]
        for j in range(4):
            m[:, j * 512:(j + 1) * 512] = np.where(j * 128 + ki <= qi, 0.0, NEG)
        MASKS_BF = m.astype(BF)
    return MASKS_BF


IDENT_F32 = np.eye(128, dtype=np.float32)


def _tile_oc(Wmat):
    n = Wmat.shape[1] // 128
    return np.ascontiguousarray(
        Wmat.reshape(16, 128, n, 128).transpose(2, 1, 0, 3).reshape(n, 128, 2048))


def _mid_weights(layer, inputs):
    if layer == 0:
        Wo = inputs["L0_attn_W_o"]
        wg, wu, wd = inputs["L0_ffn_gate_w"], inputs["L0_ffn_up_w"], inputs["L0_ffn_down_w"]
        sg, su, sd = inputs["L0_ffn_gate_scale"], inputs["L0_ffn_up_scale"], inputs["L0_ffn_down_scale"]
    else:
        Wo = inputs["L1_attn_Wo"]
        wg, wu, wd = inputs["L1_ffn_wq_gate"], inputs["L1_ffn_wq_up"], inputs["L1_ffn_wq_down"]
        sg, su, sd = inputs["L1_ffn_s_gate"], inputs["L1_ffn_s_up"], inputs["L1_ffn_s_down"]
    IC = wg.shape[0] // 128
    wo_t = np.ascontiguousarray(
        Wo.reshape(16, 128, 4, 512).transpose(2, 0, 1, 3).reshape(4, 4, 4, 128, 512)
        .transpose(0, 1, 3, 2, 4).reshape(4, 4, 128, 2048))
    wg_t = wg.reshape(IC, 128, 16, 128).transpose(0, 3, 2, 1).reshape(IC, 128, 2048)
    wu_t = wu.reshape(IC, 128, 16, 128).transpose(0, 3, 2, 1).reshape(IC, 128, 2048)
    wgu_t = np.ascontiguousarray(np.concatenate([wg_t, wu_t], axis=2))
    wd_t = np.ascontiguousarray(
        wd.T.reshape(IC // 2, 2, 128, 2, 1024).transpose(3, 0, 2, 1, 4).reshape(2, IC // 2, 128, 2048))
    d = {
        "wo_d": wo_t,
        "wgu_d": wgu_t,
        "wd_d": wd_t,
        "sg_d": np.ascontiguousarray(sg.reshape(IC, 128).T),
        "su_d": np.ascontiguousarray(su.reshape(IC, 128).T),
        "sd_d": np.ascontiguousarray(sd.reshape(1, D)),
        "idf_d": IDENT_F32,
    }
    if layer == 0:
        wa1 = np.concatenate([
            _tile_oc(inputs["L1_attn_Wq"]),
            _tile_oc(inputs["L1_attn_Wk"]),
            _tile_oc(inputs["L1_attn_Wv"])], axis=0)
        d["wa1_d"] = np.ascontiguousarray(wa1)
        for nm, key in [("bq_d", "L1_attn_bq"), ("bk_d", "L1_attn_bk"),
                        ("bv_d", "L1_attn_bv"), ("lng_d", "L1_attn_ln_g"),
                        ("lnb_d", "L1_attn_lnb" if False else "L1_attn_ln_b")]:
            d[nm] = np.ascontiguousarray(inputs[key].reshape(16, 128).T)
    else:
        d["bo_d"] = np.ascontiguousarray(inputs["L1_attn_bo"].reshape(1, D))
    return d


def _attn_in_maps(q_res, k_res, v_res_tm=None, v_res_fm=None):
    masks = _masks_bf()
    in_maps = []
    for ac in range(NCORES):
        b = ac // 4
        heads = [4 * (ac % 4) + i for i in range(NPAIR)]
        qs = np.empty((NPAIR, 128, S), BF)
        kk = np.empty((NPAIR, 128, S), BF)
        vt = np.empty((NPAIR, 128, S), BF)
        for i, h in enumerate(heads):
            qs[i] = np.concatenate([q_res[4 * b + j][h * 128:(h + 1) * 128, :] for j in range(4)], axis=1)
            kk[i] = np.concatenate([k_res[4 * b + j][h * 128:(h + 1) * 128, :] for j in range(4)], axis=1)
            if v_res_tm is not None:
                vfull = np.concatenate([v_res_tm[4 * b + j][:, h * 128:(h + 1) * 128] for j in range(4)], axis=0)
            else:
                vfull = np.concatenate([v_res_fm[4 * b + j][h * 128:(h + 1) * 128, :] for j in range(4)], axis=1).T
            vt[i] = vfull.reshape(32, 128, 128).transpose(1, 0, 2).reshape(128, S)
        in_maps.append({"qs_d": np.ascontiguousarray(qs), "kk_d": np.ascontiguousarray(kk),
                        "vt_d": np.ascontiguousarray(vt), "mk_d": masks, "id_d": IDENT_BF})
    return in_maps


def _av_for_core(avn_res, c):
    b = c // 4
    sl = slice((c % 4) * T, (c % 4 + 1) * T)
    out = np.empty((16, 128, T), BF)
    for h in range(16):
        ac = 4 * b + h // 4
        out[h] = avn_res[ac][h % 4][:, sl]
    return out


def get_prog(name):
    if name not in _cache:
        if name == "l1":
            _cache[name] = build_l1()
        elif name == "l2":
            _cache[name] = build_l2()
        elif name == "mid0":
            _cache[name] = build_mid(0)
        elif name == "mid1":
            _cache[name] = build_mid(1)
    return _cache[name]


def kernel(**inputs):
    x = np.asarray(inputs["L0_attn_x"])
    xt = np.ascontiguousarray(x.reshape(NCORES, T, D))

    wa, wv = l1_weight_layout(np.asarray(inputs["L0_attn_W_qkv"]))
    nc1 = get_prog("l1")
    im1 = [{"x_sl": xt[c], "wa": wa, "wv": wv, "ident_in": IDENT_BF}
           for c in range(NCORES)]
    r1 = run_bass_kernel_spmd(nc1, im1, core_ids=list(range(NCORES))).results

    nc2 = get_prog("l2")
    im2 = _attn_in_maps([r["q_out"] for r in r1], [r["k_out"] for r in r1],
                        v_res_tm=[r["v_out"] for r in r1])
    r2 = run_bass_kernel_spmd(nc2, im2, core_ids=list(range(NCORES))).results
    avn0 = [r["avn_d"] for r in r2]

    w0 = _mid_weights(0, inputs)
    nc3 = get_prog("mid0")
    im3 = []
    for c in range(NCORES):
        m = dict(w0)
        m["x_sl"] = xt[c]
        m["av_d"] = _av_for_core(avn0, c)
        im3.append(m)
    r3 = run_bass_kernel_spmd(nc3, im3, core_ids=list(range(NCORES))).results

    im4 = _attn_in_maps([r["q1_o"] for r in r3], [r["k1_o"] for r in r3],
                        v_res_fm=[r["v1_o"] for r in r3])
    r4 = run_bass_kernel_spmd(nc2, im4, core_ids=list(range(NCORES))).results
    avn1 = [r["avn_d"] for r in r4]

    w1 = _mid_weights(1, inputs)
    nc5 = get_prog("mid1")
    im5 = []
    for c in range(NCORES):
        m = dict(w1)
        m["x_sl"] = r3[c]["x_out"]
        m["av_d"] = _av_for_core(avn1, c)
        im5.append(m)
    r5 = run_bass_kernel_spmd(nc5, im5, core_ids=list(range(NCORES))).results

    out = np.concatenate([r5[c]["x_out"] for c in range(NCORES)], axis=0)
    return out.reshape(B, S, D)


# version 13 (final)
# speedup vs baseline: 1.0181x; 1.0181x over previous
import sys
import contextlib

sys.path.insert(0, "/opt/trn_rl_repo")

import numpy as np
import ml_dtypes

import concourse.bass as bass
import concourse.mybir as mybir
from concourse.bass_utils import run_bass_kernel_spmd

import concourse.bass_utils as _bu

if not getattr(_bu, "_ldwopt_patched", False):
    _orig_run_command = _bu.run_command

    def _run_command_ldwopt(cmd, *a, **kw):
        try:
            cmd = [("--enable-ldw-opt=true" if c == "--enable-ldw-opt=false" else c)
                   for c in cmd]
        except TypeError:
            pass
        return _orig_run_command(cmd, *a, **kw)

    _bu.run_command = _run_command_ldwopt
    _bu._ldwopt_patched = True


F32 = mybir.dt.float32
BF16 = mybir.dt.bfloat16
I32 = mybir.dt.int32
AF = mybir.ActivationFunctionType
ALU = mybir.AluOpType
AX = mybir.AxisListType
BF = ml_dtypes.bfloat16

B = 2
S = 4096
D = 2048
H = 16
HD = 128
NCORES = 8
T = (B * S) // NCORES
NT = T // 128
NPAIR = (B * H) // NCORES
EPS = 1e-5
ISQ = float(1.0 / np.sqrt(128.0))
NEG = -1.0e30

_cache = {}

ENGS = ("gpsimd", "vector", "scalar", "tensor", "sync")


class Plan:

    def __init__(self):
        self.ops = {e: [] for e in ENGS}

    def add(self, eng, fn):
        self.ops[eng].append(fn)

    def run(self, nc):
        with nc.Block() as block:
            @block.gpsimd
            def _(g):
                for f in self.ops["gpsimd"]:
                    f(g)

            @block.vector
            def _(v):
                for f in self.ops["vector"]:
                    f(v)

            @block.scalar
            def _(s):
                for f in self.ops["scalar"]:
                    f(s)

            @block.tensor
            def _(t):
                for f in self.ops["tensor"]:
                    f(t)

            @block.sync
            def _(sy):
                for f in self.ops["sync"]:
                    f(sy)


class Ctr:
    def __init__(self):
        self.v = 0

    def inc(self, n=1):
        self.v += n
        return self.v



class SlotSem:

    def __init__(self, es, nc, name):
        self.sem = es.enter_context(nc.semaphore(name))
        self.c = Ctr()

    def inc(self):
        return self.c.inc(16)

    def total(self):
        return self.c.v


def W(p, eng, sem, val):
    if val > 0:
        p.add(eng, lambda e, sem=sem, val=val: e.wait_ge(sem, val))



def build_l1():
    nc = bass.Bass("TRN2", target_bir_lowering=False, debug=False)
    x_sl = nc.dram_tensor("x_sl", [T, D], F32, kind="ExternalInput").ap()
    wa = nc.dram_tensor("wa", [32, 128, 16 * 128], F32, kind="ExternalInput").ap()
    wv = nc.dram_tensor("wv", [16, 128, D], F32, kind="ExternalInput").ap()
    ident_in = nc.dram_tensor("ident_in", [128, 128], BF16, kind="ExternalInput").ap()
    q_out = nc.dram_tensor("q_out", [D, T], BF16, kind="ExternalOutput").ap()
    k_out = nc.dram_tensor("k_out", [D, T], BF16, kind="ExternalOutput").ap()
    v_out = nc.dram_tensor("v_out", [T, D], BF16, kind="ExternalOutput").ap()

    es = contextlib.ExitStack()
    with es:
        xt = [es.enter_context(nc.sbuf_tensor(f"xt{i}", [128, D], F32)) for i in range(2)]
        stat = es.enter_context(nc.sbuf_tensor("stat", [128, 16], F32))
        ceps = es.enter_context(nc.sbuf_tensor("ceps", [128, 1], F32))
        xn2t = [es.enter_context(nc.sbuf_tensor(f"xn2t{i}", [128, D], BF16)) for i in range(2)]
        xn_fm = es.enter_context(nc.sbuf_tensor("xn_fm", [128, 16 * T], BF16))
        ident = es.enter_context(nc.sbuf_tensor("ident", [128, 128], BF16))
        wqk = [es.enter_context(nc.sbuf_tensor(f"wqk{i}", [128, 2048], BF16)) for i in range(2)]
        wvb = es.enter_context(nc.sbuf_tensor("wvb", [128, 16 * D], BF16))
        stg = [es.enter_context(nc.sbuf_tensor(f"stg{i}", [128, T], BF16)) for i in range(2)]
        vstg = [es.enter_context(nc.sbuf_tensor(f"vstg{i}", [128, D], BF16)) for i in range(2)]
        junk = es.enter_context(nc.sbuf_tensor("junk", [128, D], BF16))
        ptr = [es.enter_context(nc.psum_tensor(f"ptr{i}", [128, 128], BF16)) for i in range(2)]
        pmm = [es.enter_context(nc.psum_tensor(f"pmm{i}", [128, 512], F32)) for i in range(4)]

        lc = SlotSem(es, nc, "lc")
        lx0 = SlotSem(es, nc, "lx0")
        lx = [lx0, lx0]
        lv = SlotSem(es, nc, "lv")
        lw = [SlotSem(es, nc, f"lw{i}") for i in range(2)]
        sq = [SlotSem(es, nc, f"sq{i}") for i in range(2)]
        sv = [SlotSem(es, nc, f"sv{i}") for i in range(2)]
        ve = es.enter_context(nc.semaphore("ve"))
        se = es.enter_context(nc.semaphore("se"))
        te = es.enter_context(nc.semaphore("te"))
        cve, cse, cte = Ctr(), Ctr(), Ctr()

        p = Plan()

        p.add("gpsimd", lambda g: g.dma_start(out=ident[:], in_=ident_in).then_inc(lc.sem, 16))
        ld_ident = lc.inc()
        p.add("vector", lambda v: v.memset(ceps[:], EPS).then_inc(ve, 1))
        cve.inc()

        ve_trev = {}
        astate = {}
        xt_guard = {0: 0, 1: 0}
        xn_guard = {0: 0, 1: 0}
        sq_guard = {0: 0, 1: 0}

        def a_stageA(tt):
            sl = tt % 2
            st_ = {}
            if xt_guard[sl]:
                W(p, "gpsimd", ve, xt_guard[sl])
            p.add("gpsimd", lambda g, tt=tt, sl=sl: g.dma_start(
                out=xt[sl][:], in_=x_sl[tt * 128:(tt + 1) * 128, :]).then_inc(lx[sl].sem, 16))
            ld_x = lx[sl].inc()
            st_["ld_x"] = ld_x
            W(p, "vector", lx[sl].sem, ld_x)
            p.add("vector", lambda v, sl=sl: v.tensor_reduce(
                stat[:, sl:sl + 1], xt[sl][:], AX.X, ALU.add).then_inc(ve, 1))
            st_["ve_red"] = cve.inc()
            W(p, "scalar", lx[sl].sem, ld_x)
            if sq_guard[sl]:
                W(p, "scalar", ve, sq_guard[sl])
            p.add("scalar", lambda s, sl=sl: s.activation(
                junk[:], xt[sl][:], AF.Square, accum_out=stat[:, 2 + sl:3 + sl]).then_inc(se, 1))
            st_["se_sq"] = cse.inc()
            astate[tt] = st_

        def a_stageB(tt):
            sl = tt % 2
            st_ = astate[tt]
            W(p, "vector", ve, st_["ve_red"])
            W(p, "vector", se, st_["se_sq"])
            p.add("vector", lambda v, sl=sl: v.tensor_scalar(
                stat[:, 4 + sl:5 + sl], stat[:, sl:sl + 1], 1.0 / D, None,
                ALU.mult).then_inc(ve, 1))
            ve_m = cve.inc()
            W(p, "vector", ve, ve_m)
            p.add("vector", lambda v, sl=sl: v.tensor_tensor(
                stat[:, 6 + sl:7 + sl], stat[:, 4 + sl:5 + sl], stat[:, 4 + sl:5 + sl],
                ALU.mult).then_inc(ve, 1))
            ve_m2 = cve.inc()
            W(p, "vector", ve, ve_m2)
            p.add("vector", lambda v, sl=sl: v.tensor_scalar(
                stat[:, 8 + sl:9 + sl], stat[:, 2 + sl:3 + sl], 1.0 / D,
                stat[:, 6 + sl:7 + sl], ALU.mult, ALU.subtract).then_inc(ve, 1))
            ve_var = cve.inc()
            sq_guard[sl] = ve_var
            W(p, "scalar", ve, ve_var)
            p.add("scalar", lambda s, sl=sl: s.activation(
                stat[:, 10 + sl:11 + sl], stat[:, 8 + sl:9 + sl], AF.Ln,
                bias=ceps[:]).then_inc(se, 1))
            se_ln = cse.inc()
            W(p, "scalar", se, se_ln)
            p.add("scalar", lambda s, sl=sl: s.activation(
                stat[:, 12 + sl:13 + sl], stat[:, 10 + sl:11 + sl], AF.Exp,
                scale=-0.5).then_inc(se, 1))
            se_rs = cse.inc()
            W(p, "vector", se, se_rs)
            if xn_guard[sl]:
                W(p, "vector", te, xn_guard[sl])
            p.add("vector", lambda v, sl=sl: v.tensor_scalar(
                xn2t[sl][:], xt[sl][:], stat[:, 4 + sl:5 + sl], stat[:, 12 + sl:13 + sl],
                ALU.subtract, ALU.mult).then_inc(ve, 1))
            st_["ve_xn"] = cve.inc()
            xt_guard[sl] = st_["ve_xn"]

        def a_stageC(tt):
            sl = tt % 2
            st_ = astate[tt]
            W(p, "tensor", lc.sem, ld_ident)
            W(p, "tensor", ve, st_["ve_xn"])
            te_tr_last = 0
            for dc in range(16):
                bk = dc % 2
                prev = (tt, dc - 2) if dc >= 2 else ((tt - 1, 14 + dc) if tt >= 1 else None)
                if prev is not None and prev in ve_trev:
                    W(p, "tensor", ve, ve_trev[prev])
                p.add("tensor", lambda t, tt=tt, dc=dc, sl=sl, bk=bk: t.transpose(
                    ptr[bk][:], xn2t[sl][:, dc * 128:(dc + 1) * 128], ident[:]).then_inc(te, 1))
                te_tr = cte.inc()
                te_tr_last = te_tr
                W(p, "vector", te, te_tr)
                p.add("vector", lambda v, tt=tt, dc=dc, bk=bk: v.tensor_copy(
                    xn_fm[:, dc * T + tt * 128: dc * T + (tt + 1) * 128],
                    ptr[bk][:]).then_inc(ve, 1))
                ve_trev[(tt, dc)] = cve.inc()
            xn_guard[sl] = te_tr_last
            del astate[tt]

        for step in range(NT + 2):
            if step >= 2:
                a_stageC(step - 2)
            if 1 <= step <= NT:
                a_stageB(step - 1)
            if step < NT:
                a_stageA(step)
        ve_phaseA = cve.v

        for i in range(16):
            p.add("gpsimd", lambda g, i=i: g.dma_start(
                out=wvb[:, i * D:(i + 1) * D], in_=wv[i]).then_inc(lv.sem, 16))
            lv.inc()
        ld_wv = lv.total()

        te_qk = {}
        se_evac = {}
        so_qk = {}
        bankprev = {}
        for oc in range(32):
            wsl = oc % 2
            if oc >= 2:
                W(p, "gpsimd", te, te_qk[oc - 2])
            p.add("gpsimd", lambda g, oc=oc, wsl=wsl: g.dma_start(
                out=wqk[wsl][:], in_=wa[oc]).then_inc(lw[wsl].sem, 16))
            ld_w = lw[wsl].inc()
            for tg in range(2):
                bank = (2 * oc + tg) % 4
                W(p, "tensor", lw[wsl].sem, ld_w)
                W(p, "tensor", ve, ve_phaseA)
                if bank in bankprev:
                    W(p, "tensor", se, bankprev[bank])
                for ic in range(16):
                    def mmqk(t, oc=oc, tg=tg, ic=ic, bank=bank, wsl=wsl):
                        ins = t.matmul(
                            pmm[bank][:], wqk[wsl][:, ic * 128:(ic + 1) * 128],
                            xn_fm[:, ic * T + tg * 512: ic * T + tg * 512 + 512],
                            start=(ic == 0), stop=(ic == 15), skip_group_check=True)
                        if ic == 15:
                            ins.then_inc(te, 1)
                    p.add("tensor", mmqk)
                te_mm = cte.inc()
                if tg == 1:
                    te_qk[oc] = te_mm
                W(p, "scalar", te, te_mm)
                if oc >= 2 and tg == 0:
                    W(p, "scalar", sq[wsl].sem, so_qk[oc - 2])
                p.add("scalar", lambda s, oc=oc, tg=tg, bank=bank, wsl=wsl: s.activation(
                    stg[wsl][:, tg * 512:tg * 512 + 512], pmm[bank][:], AF.Copy,
                    scale=(ISQ if oc < 16 else 1.0)).then_inc(se, 1))
                se_evac[(oc, tg)] = cse.inc()
                bankprev[bank] = se_evac[(oc, tg)]
            W(p, "sync", se, se_evac[(oc, 1)])
            dst = q_out if oc < 16 else k_out
            r0 = (oc % 16) * 128
            p.add("sync", lambda sy, oc=oc, dst=dst, r0=r0, wsl=wsl: sy.dma_start(
                out=dst[r0:r0 + 128, :], in_=stg[wsl][:]).then_inc(sq[wsl].sem, 16))
            so_qk[oc] = sq[wsl].inc()

        ve_vevac = {}
        so_v = {}
        for tt in range(NT):
            vsl = tt % 2
            for og in range(4):
                bank = og
                W(p, "tensor", lv.sem, ld_wv)
                if tt == 0:
                    W(p, "tensor", se, bankprev[bank])
                prevkey = (tt - 1, og)
                if prevkey in ve_vevac:
                    W(p, "tensor", ve, ve_vevac[prevkey])
                for ic in range(16):
                    def mmv(t, tt=tt, og=og, ic=ic, bank=bank):
                        ins = t.matmul(
                            pmm[bank][:], xn_fm[:, ic * T + tt * 128: ic * T + (tt + 1) * 128],
                            wvb[:, ic * D + og * 512: ic * D + og * 512 + 512],
                            start=(ic == 0), stop=(ic == 15), skip_group_check=True)
                        if ic == 15:
                            ins.then_inc(te, 1)
                    p.add("tensor", mmv)
                te_v = cte.inc()
                W(p, "vector", te, te_v)
                if tt >= 2 and og == 0:
                    W(p, "vector", sv[vsl].sem, so_v[tt - 2])
                p.add("vector", lambda v, tt=tt, og=og, bank=bank, vsl=vsl: v.tensor_copy(
                    vstg[vsl][:, og * 512:og * 512 + 512], pmm[bank][:]).then_inc(ve, 1))
                ve_vevac[(tt, og)] = cve.inc()
            W(p, "sync", ve, ve_vevac[(tt, 3)])
            p.add("sync", lambda sy, tt=tt, vsl=vsl: sy.dma_start(
                out=v_out[tt * 128:(tt + 1) * 128, :], in_=vstg[vsl][:]).then_inc(sv[vsl].sem, 16))
            so_v[tt] = sv[vsl].inc()

        for ss in (sq[0], sq[1], sv[0], sv[1]):
            W(p, "sync", ss.sem, ss.total())

        p.run(nc)
    return nc



def l1_weight_layout(W_qkv):
    wqk = W_qkv[:, :2 * D]
    wa = np.ascontiguousarray(
        wqk.reshape(16, 128, 32, 128).transpose(2, 1, 0, 3).reshape(32, 128, 16 * 128))
    wv = np.ascontiguousarray(W_qkv[:, 2 * D:].reshape(16, 128, D))
    return wa, wv


IDENT_BF = np.eye(128, dtype=np.float32).astype(BF)



def build_l2():
    nc = bass.Bass("TRN2", target_bir_lowering=False, debug=False)
    qs_d = nc.dram_tensor("qs_d", [NPAIR, 128, S], BF16, kind="ExternalInput").ap()
    kk_d = nc.dram_tensor("kk_d", [NPAIR, 128, S], BF16, kind="ExternalInput").ap()
    vt_d = nc.dram_tensor("vt_d", [NPAIR, 128, S], BF16, kind="ExternalInput").ap()
    mk_d = nc.dram_tensor("mk_d", [128, 4 * 512], BF16, kind="ExternalInput").ap()
    id_d = nc.dram_tensor("id_d", [128, 128], BF16, kind="ExternalInput").ap()
    avn_d = nc.dram_tensor("avn_d", [NPAIR, 128, S], BF16, kind="ExternalOutput").ap()

    NKT = S // 128
    NQC = S // 512
    es = contextlib.ExitStack()
    with es:
        qb = [es.enter_context(nc.sbuf_tensor(f"qb{i}", [128, S], BF16)) for i in range(NPAIR)]
        kb = [es.enter_context(nc.sbuf_tensor(f"kb{i}", [128, S], BF16)) for i in range(NPAIR)]
        vb = [es.enter_context(nc.sbuf_tensor(f"vb{i}", [128, S], BF16)) for i in range(NPAIR)]
        mkb = es.enter_context(nc.sbuf_tensor("mkb", [128, 4 * 512], BF16))
        idb = es.enter_context(nc.sbuf_tensor("idb", [128, 128], BF16))
        ones_bf = es.enter_context(nc.sbuf_tensor("ones_bf", [128, 1], BF16))
        ones_f = es.enter_context(nc.sbuf_tensor("ones_f", [1, 128], F32))
        at = es.enter_context(nc.sbuf_tensor("at", [128, 4 * 512], BF16))
        sq2 = [es.enter_context(nc.sbuf_tensor(f"sq2_{i}", [128, 512], BF16)) for i in range(2)]
        nrow = es.enter_context(nc.sbuf_tensor("nrow", [1, 2 * NQC * 512], F32))
        mrow = es.enter_context(nc.sbuf_tensor("mrow", [1, 8], F32))
        negm = es.enter_context(nc.sbuf_tensor("negm", [128, NPAIR], F32))
        rz = [es.enter_context(nc.sbuf_tensor(f"rz{i}", [1, 512], F32)) for i in range(2)]
        rzb = es.enter_context(nc.sbuf_tensor("rzb", [128, 512], F32))
        avs = [es.enter_context(nc.sbuf_tensor(f"avs{i}", [128, 512], BF16)) for i in range(2)]
        ps = [es.enter_context(nc.psum_tensor(f"ps{i}", [128, 512], F32)) for i in range(2)]
        pav = [es.enter_context(nc.psum_tensor(f"pav{i}", [128, 512], F32)) for i in range(2)]
        pz = [es.enter_context(nc.psum_tensor(f"pz{i}", [1, 512], F32)) for i in range(2)]
        przb = es.enter_context(nc.psum_tensor("przb", [128, 512], F32))
        pn2 = es.enter_context(nc.psum_tensor("pn2", [1, 512], F32))

        lc = SlotSem(es, nc, "lc")
        sa = [SlotSem(es, nc, f"sa{i}") for i in range(2)]
        ve = es.enter_context(nc.semaphore("ve"))
        se = es.enter_context(nc.semaphore("se"))
        te = es.enter_context(nc.semaphore("te"))
        cve, cse, cte = Ctr(), Ctr(), Ctr()
        p = Plan()

        p.add("gpsimd", lambda g: g.dma_start(out=mkb[:], in_=mk_d).then_inc(lc.sem, 16))
        lc.inc()
        p.add("gpsimd", lambda g: g.dma_start(out=idb[:], in_=id_d).then_inc(lc.sem, 16))
        lc.inc()
        for pp in range(NPAIR):
            p.add("gpsimd", lambda g, pp=pp: g.dma_start(out=qb[pp][:], in_=qs_d[pp]).then_inc(lc.sem, 16))
            lc.inc()
            p.add("gpsimd", lambda g, pp=pp: g.dma_start(out=kb[pp][:], in_=kk_d[pp]).then_inc(lc.sem, 16))
            lc.inc()
            p.add("gpsimd", lambda g, pp=pp: g.dma_start(out=vb[pp][:], in_=vt_d[pp]).then_inc(lc.sem, 16))
            lc.inc()
        lc_all = lc.total()
        p.add("vector", lambda v: v.memset(ones_bf[:], 1.0).then_inc(ve, 1))
        ve_ones = cve.inc()
        p.add("vector", lambda v: v.memset(ones_f[:], 1.0).then_inc(ve, 1))
        ve_onesf = cve.inc()

        se_sq_n = 0
        ve_nev_prev = 0
        te_pn_hist = {}
        ve_cp_hist = {}
        pnb = [przb[0:1, 0:512], pn2[0:1, 0:512]]
        for pp in range(NPAIR):
            for which, buf in ((0, qb[pp]), (1, kb[pp])):
                for qg in range(NQC):
                    i = se_sq_n
                    sl = i % 2
                    if i == 0:
                        W(p, "scalar", lc.sem, lc_all)
                    if i >= 2:
                        W(p, "scalar", te, te_pn_hist[i - 2])
                    p.add("scalar", lambda s, buf=buf, qg=qg, sl=sl: s.activation(
                        sq2[sl][:], buf[:, qg * 512:(qg + 1) * 512], AF.Square).then_inc(se, 1))
                    se_sq = cse.inc()
                    se_sq_n += 1
                    W(p, "tensor", se, se_sq)
                    if i == 0:
                        W(p, "tensor", ve, ve_ones)
                        W(p, "tensor", lc.sem, lc_all)
                    if i >= 2:
                        W(p, "tensor", ve, ve_cp_hist[i - 2])
                    if which == 0 and qg == 0 and ve_nev_prev:
                        W(p, "tensor", ve, ve_nev_prev)
                    p.add("tensor", lambda t, sl=sl: t.matmul(
                        pnb[sl], ones_bf[:], sq2[sl][:], start=True, stop=True,
                        skip_group_check=True).then_inc(te, 1))
                    te_pn = cte.inc()
                    te_pn_hist[i] = te_pn
                    W(p, "vector", te, te_pn)
                    col = (which * NQC + qg) * 512
                    p.add("vector", lambda v, col=col, sl=sl: v.tensor_copy(
                        nrow[:, col:col + 512], pnb[sl]).then_inc(ve, 1))
                    ve_nev = cve.inc()
                    ve_cp_hist[i] = ve_nev
                    ve_nev_prev = ve_nev
            W(p, "vector", ve, ve_nev_prev)
            p.add("vector", lambda v: v.tensor_reduce(
                mrow[:, 0:1], nrow[:, 0:NQC * 512], AX.X, ALU.max).then_inc(ve, 1))
            cve.inc()
            p.add("vector", lambda v: v.tensor_reduce(
                mrow[:, 1:2], nrow[:, NQC * 512:], AX.X, ALU.max).then_inc(ve, 1))
            ve_mk2 = cve.inc()
            W(p, "vector", ve, ve_mk2)
            p.add("vector", lambda v: v.tensor_tensor(
                mrow[:, 2:3], mrow[:, 0:1], mrow[:, 1:2], ALU.mult).then_inc(ve, 1))
            ve_mm = cve.inc()
            W(p, "scalar", ve, ve_mm)
            p.add("scalar", lambda s: s.activation(
                mrow[:, 3:4], mrow[:, 2:3], AF.Sqrt).then_inc(se, 1))
            se_m = cse.inc()
            W(p, "vector", se, se_m)
            p.add("vector", lambda v: v.tensor_scalar(
                mrow[:, 4:5], mrow[:, 3:4], -1.02, None, ALU.mult).then_inc(ve, 1))
            ve_negm = cve.inc()
            W(p, "tensor", ve, ve_negm)
            W(p, "tensor", ve, ve_onesf)
            p.add("tensor", lambda t: t.matmul(
                przb[:, 0:1], ones_f[:], mrow[:, 4:5], start=True, stop=True,
                skip_group_check=True).then_inc(te, 1))
            te_bc = cte.inc()
            W(p, "vector", te, te_bc)
            p.add("vector", lambda v, pp=pp: v.tensor_copy(
                negm[:, pp:pp + 1], przb[:, 0:1]).then_inc(ve, 1))
            ve_negm2 = cve.inc()
            ve_nev_prev = ve_negm2
        ve_prelude = cve.v

        slot_free = {}
        s_bankfree = {0: 0, 1: 0}
        bankfree = {}
        so_avs = {}
        gqc = 0
        skt = 0
        pending_tail = [None]
        pending_epi = [None]

        for pp in range(NPAIR):
            for qc in range(NQC):
                nkt = 4 * qc + 4
                pavb = pav[gqc % 2]
                pzb = pz[gqc % 2]
                events = {}

                def emit_S(kt, pp=None, qc=None):
                    nonlocal skt
                    sbank = skt % 2
                    events[("sbank", kt)] = sbank
                    skt += 1
                    if gqc == 0 and kt == 0:
                        W(p, "tensor", lc.sem, lc_all)
                    if s_bankfree[sbank]:
                        W(p, "tensor", se, s_bankfree[sbank])
                    diag = kt >= 4 * qc
                    j = kt - 4 * qc
                    def mmS(t, pp=pp, qc=qc, kt=kt, sbank=sbank, diag=diag):
                        ins = t.matmul(
                            ps[sbank][:], kb[pp][:, kt * 128:(kt + 1) * 128],
                            qb[pp][:, qc * 512:(qc + 1) * 512],
                            start=True, stop=not diag, skip_group_check=True)
                        if not diag:
                            ins.then_inc(te, 1)
                    p.add("tensor", mmS)
                    if diag:
                        p.add("tensor", lambda t, sbank=sbank, j=j: t.matmul(
                            ps[sbank][:], idb[:], mkb[:, j * 512:(j + 1) * 512],
                            start=False, stop=True, skip_group_check=True).then_inc(te, 1))
                    events[("te_s", kt)] = cte.inc()

                def emit_exp(kt, pp=None, first=False):
                    sbank = events[("sbank", kt)]
                    slot = kt % 4
                    W(p, "scalar", te, events[("te_s", kt)])
                    if slot in slot_free:
                        W(p, "scalar", te, slot_free[slot])
                    if first:
                        W(p, "scalar", ve, ve_prelude)
                    p.add("scalar", lambda s, sbank=sbank, slot=slot, pp=pp: s.activation(
                        at[:, slot * 512:(slot + 1) * 512], ps[sbank][:], AF.Exp,
                        bias=negm[:, pp:pp + 1]).then_inc(se, 1))
                    ev = cse.inc()
                    events[("se_exp", kt)] = ev
                    s_bankfree[sbank] = ev

                def emit_AVZ(kt, pp=None, nkt=None, pavb=None, pzb=None, events=None, g=None):
                    slot = kt % 4
                    W(p, "tensor", se, events[("se_exp", kt)])
                    if kt == 0:
                        for bname in (("pav", g % 2), ("pz", g % 2)):
                            if bname in bankfree:
                                sem, val = bankfree[bname]
                                W(p, "tensor", sem, val)
                    p.add("tensor", lambda t, pp=pp, kt=kt, slot=slot, nkt=nkt, pavb=pavb: t.matmul(
                        pavb[:], vb[pp][:, kt * 128:(kt + 1) * 128],
                        at[:, slot * 512:(slot + 1) * 512],
                        start=(kt == 0), stop=(kt == nkt - 1), skip_group_check=True))
                    p.add("tensor", lambda t, kt=kt, slot=slot, nkt=nkt, pzb=pzb: t.matmul(
                        pzb[:], ones_bf[:], at[:, slot * 512:(slot + 1) * 512],
                        start=(kt == 0), stop=(kt == nkt - 1),
                        skip_group_check=True).then_inc(te, 1))
                    ev = cte.inc()
                    events[("te_avz", kt)] = ev
                    slot_free[slot] = ev

                def make_tail(pp, qc, nkt, pavb, pzb, events, g):
                    def tail():
                        emit_AVZ(nkt - 2, pp=pp, nkt=nkt, pavb=pavb, pzb=pzb, events=events, g=g)
                        emit_AVZ(nkt - 1, pp=pp, nkt=nkt, pavb=pavb, pzb=pzb, events=events, g=g)
                        te_last = events[("te_avz", nkt - 1)]
                        W(p, "scalar", te, te_last)
                        rzslot = g % 2
                        if ("rz", rzslot) in bankfree:
                            sem, val = bankfree[("rz", rzslot)]
                            W(p, "scalar", sem, val)
                        p.add("scalar", lambda s, pzb=pzb, rzslot=rzslot: s.activation(
                            rz[rzslot][:], pzb[:], AF.Ln).then_inc(se, 1))
                        se_lnz = cse.inc()
                        bankfree[("pz", g % 2)] = (se, se_lnz)
                        W(p, "scalar", se, se_lnz)
                        p.add("scalar", lambda s, rzslot=rzslot: s.activation(
                            rz[rzslot][:], rz[rzslot][:], AF.Exp, scale=-1.0).then_inc(se, 1))
                        rz_evt = ("se", cse.inc())
                        pending_epi[0] = make_epi(g, pp, qc, pavb, rz_evt, rzslot)
                    return tail

                def make_epi(g, pp, qc, pavb, rz_evt, rzslot):
                    def epi():
                        W(p, "tensor", se, rz_evt[1])
                        if "przb" in bankfree:
                            sem, val = bankfree["przb"]
                            W(p, "tensor", sem, val)
                        p.add("tensor", lambda t, rzslot=rzslot: t.matmul(
                            przb[:], ones_f[:], rz[rzslot][:], start=True, stop=True,
                            skip_group_check=True).then_inc(te, 1))
                        te_rzb = cte.inc()
                        bankfree[("rz", rzslot)] = (te, te_rzb)
                        W(p, "vector", te, te_rzb)
                        p.add("vector", lambda v: v.tensor_copy(rzb[:], przb[:]).then_inc(ve, 1))
                        ve_cp = cve.inc()
                        bankfree["przb"] = (ve, ve_cp)
                        W(p, "vector", ve, ve_cp)
                        if g >= 2 and (g - 2) in so_avs:
                            W(p, "vector", sa[g % 2].sem, so_avs[g - 2])
                        p.add("vector", lambda v, g=g, pavb=pavb: v.tensor_tensor(
                            avs[g % 2][:], pavb[:], rzb[:], ALU.mult).then_inc(ve, 1))
                        ve_nrm = cve.inc()
                        bankfree[("pav", g % 2)] = (ve, ve_nrm)
                        W(p, "sync", ve, ve_nrm)
                        p.add("sync", lambda sy, pp=pp, qc=qc, g=g: sy.dma_start(
                            out=avn_d[pp][0:128, qc * 512:(qc + 1) * 512],
                            in_=avs[g % 2][:]).then_inc(sa[g % 2].sem, 16))
                        so_avs[g] = sa[g % 2].inc()
                    return epi

                emit_S(0, pp=pp, qc=qc)
                emit_exp(0, pp=pp, first=(gqc == 0))
                emit_S(1, pp=pp, qc=qc)
                emit_exp(1, pp=pp)
                if pending_tail[0] is not None:
                    pending_tail[0]()
                    pending_tail[0] = None
                epi_flush_at = min(5, nkt - 1)
                for kt in range(2, nkt):
                    emit_S(kt, pp=pp, qc=qc)
                    emit_exp(kt, pp=pp)
                    if kt >= 4:
                        emit_AVZ(kt - 4 + 2, pp=pp, nkt=nkt, pavb=pavb, pzb=pzb,
                                 events=events, g=gqc) if False else None
                    if kt == epi_flush_at and pending_epi[0] is not None:
                        pending_epi[0]()
                        pending_epi[0] = None
                    emit_AVZ(kt - 2, pp=pp, nkt=nkt, pavb=pavb, pzb=pzb,
                             events=events, g=gqc)
                if pending_epi[0] is not None:
                    pending_epi[0]()
                    pending_epi[0] = None
                pending_tail[0] = make_tail(pp, qc, nkt, pavb, pzb, events, gqc)
                gqc += 1
        if pending_tail[0] is not None:
            pending_tail[0]()
            pending_tail[0] = None
        if pending_epi[0] is not None:
            pending_epi[0]()
            pending_epi[0] = None
        for ss in (sa[0], sa[1]):
            W(p, "sync", ss.sem, ss.total())
        p.run(nc)
    return nc



def build_mid(layer):
    IC = 64 if layer == 0 else 44
    nc = bass.Bass("TRN2", target_bir_lowering=False, debug=False)
    x_sl = nc.dram_tensor("x_sl", [T, D], F32, kind="ExternalInput").ap()
    av_d = nc.dram_tensor("av_d", [16, 128, T], BF16, kind="ExternalInput").ap()
    wo_d = nc.dram_tensor("wo_d", [4, 4, 128, 4 * 512], F32, kind="ExternalInput").ap()
    wgu_d = nc.dram_tensor("wgu_d", [IC, 128, 4096], I32, kind="ExternalInput").ap()
    wd_d = nc.dram_tensor("wd_d", [2, IC // 2, 128, 2048], I32, kind="ExternalInput").ap()
    sg_d = nc.dram_tensor("sg_d", [128, IC], F32, kind="ExternalInput").ap()
    su_d = nc.dram_tensor("su_d", [128, IC], F32, kind="ExternalInput").ap()
    sd_d = nc.dram_tensor("sd_d", [1, D], F32, kind="ExternalInput").ap()
    idf_d = nc.dram_tensor("idf_d", [128, 128], F32, kind="ExternalInput").ap()
    if layer == 0:
        wa1_d = nc.dram_tensor("wa1_d", [48, 128, 2048], F32, kind="ExternalInput").ap()
        bq_d = nc.dram_tensor("bq_d", [128, 16], F32, kind="ExternalInput").ap()
        bk_d = nc.dram_tensor("bk_d", [128, 16], F32, kind="ExternalInput").ap()
        bv_d = nc.dram_tensor("bv_d", [128, 16], F32, kind="ExternalInput").ap()
        lng_d = nc.dram_tensor("lng_d", [128, 16], F32, kind="ExternalInput").ap()
        lnb_d = nc.dram_tensor("lnb_d", [128, 16], F32, kind="ExternalInput").ap()
        q1_o = nc.dram_tensor("q1_o", [D, T], BF16, kind="ExternalOutput").ap()
        k1_o = nc.dram_tensor("k1_o", [D, T], BF16, kind="ExternalOutput").ap()
        v1_o = nc.dram_tensor("v1_o", [D, T], BF16, kind="ExternalOutput").ap()
    else:
        bo_d = nc.dram_tensor("bo_d", [1, D], F32, kind="ExternalInput").ap()
    x_out = nc.dram_tensor("x_out", [T, D], F32, kind="ExternalOutput").ap()
    x1_scr = nc.dram_tensor("x1_scr", [T, D], F32, kind="Internal").ap()

    es = contextlib.ExitStack()
    with es:
        A0, B0, C0 = 0, 16384, 32768
        BBW = C0 + IC * 512
        bb = es.enter_context(nc.sbuf_tensor("bb", [128, BBW], BF16))
        xt0 = es.enter_context(nc.sbuf_tensor("xt0", [128, D], F32))
        xt = [xt0, xt0]
        z32 = es.enter_context(nc.sbuf_tensor("z32", [128, D], F32))
        z32b = es.enter_context(nc.sbuf_tensor("z32b", [128, D], F32))
        zb2 = [z32, z32b]
        wob = es.enter_context(nc.sbuf_tensor("wob", [128, 2 * 2048], BF16))
        wgub = [es.enter_context(nc.sbuf_tensor(f"wgub{i}", [128, 4096], BF16)) for i in range(2)]
        wdb = [es.enter_context(nc.sbuf_tensor(f"wdb{i}", [128, 2048], BF16)) for i in range(2)]
        sgb = es.enter_context(nc.sbuf_tensor("sgb", [128, IC], F32))
        sub = es.enter_context(nc.sbuf_tensor("sub", [128, IC], F32))
        sdr = es.enter_context(nc.sbuf_tensor("sdr", [1, D], F32))
        sdb = es.enter_context(nc.sbuf_tensor("sdb", [128, D], BF16))
        idf = es.enter_context(nc.sbuf_tensor("idf", [128, 128], F32))
        ones_bf = es.enter_context(nc.sbuf_tensor("ones_bf", [128, 1], BF16))
        ones_f = es.enter_context(nc.sbuf_tensor("ones_f", [1, 128], F32))
        ones5f = es.enter_context(nc.sbuf_tensor("ones5f", [1, 512], F32))
        cneg = es.enter_context(nc.sbuf_tensor("cneg", [1, 128], BF16))
        ceps = es.enter_context(nc.sbuf_tensor("ceps", [128, 1], F32))
        stat = es.enter_context(nc.sbuf_tensor("stat", [128, 36], F32))
        xs_bf = es.enter_context(nc.sbuf_tensor("xs_bf", [1, 512], BF16))
        hs_f = es.enter_context(nc.sbuf_tensor("hs_f", [1, 512], F32))
        sgt_t = es.enter_context(nc.sbuf_tensor("sgt_t", [128, 512], BF16))
        ut_t = es.enter_context(nc.sbuf_tensor("ut_t", [128, 512], BF16))
        sgt = sgt_t[:]
        ut = ut_t[:]
        if layer == 0:
            wab = [bb[:, A0 + i * 2048: A0 + (i + 1) * 2048] for i in range(2)]
            stg = [bb[:, A0 + 4096 + i * T: A0 + 4096 + (i + 1) * T] for i in range(2)]
            bqs = es.enter_context(nc.sbuf_tensor("bqs", [128, 16], F32))
            bkb = es.enter_context(nc.sbuf_tensor("bkb", [128, 16], F32))
            bvb = es.enter_context(nc.sbuf_tensor("bvb", [128, 16], F32))
            lng = es.enter_context(nc.sbuf_tensor("lng", [128, 16], F32))
            lnb = es.enter_context(nc.sbuf_tensor("lnb", [128, 16], F32))
        else:
            bo_bf = es.enter_context(nc.sbuf_tensor("bo_bf", [1, D], BF16))
            ones_r = es.enter_context(nc.sbuf_tensor("ones_r", [1, 128], BF16))
        pb = [es.enter_context(nc.psum_tensor(f"pb{i}", [128, 512], F32)) for i in range(8)]

        lc = SlotSem(es, nc, "lc")
        lwo = [SlotSem(es, nc, f"lwo{i}") for i in range(2)]
        lx0 = SlotSem(es, nc, "lx0")
        lx = [lx0, lx0]
        lgu = [SlotSem(es, nc, f"lgu{i}") for i in range(2)]
        lguB = [SlotSem(es, nc, f"lguB{i}") for i in range(2)]
        lwd = [SlotSem(es, nc, f"lwd{i}") for i in range(2)]
        sx1 = SlotSem(es, nc, "sx1")
        sxo = SlotSem(es, nc, "sxo")
        if layer == 0:
            lwa = [SlotSem(es, nc, f"lwa{i}") for i in range(2)]
            sst = [SlotSem(es, nc, f"sst{i}") for i in range(2)]
        ve = es.enter_context(nc.semaphore("ve"))
        se = es.enter_context(nc.semaphore("se"))
        te = es.enter_context(nc.semaphore("te"))
        sems = {"ve": ve, "se": se, "te": te, "sx1": sx1.sem, "sxo": sxo.sem,
                "lx0": lx0.sem, "lx1": lx0.sem}
        cve, cse, cte = Ctr(), Ctr(), Ctr()
        p = Plan()
        bankfree = {}

        def gwait(eng, guard):
            if guard is not None and guard[1] > 0:
                W(p, eng, sems[guard[0]], guard[1])

        def bwait(bk, eng="tensor"):
            gwait(eng, bankfree.get(bk))

        for dc in range(16):
            p.add("gpsimd", lambda g, dc=dc: g.dma_start(
                out=bb[:, C0 + dc * T: C0 + (dc + 1) * T], in_=av_d[dc]).then_inc(lc.sem, 16))
            lc.inc()
        consts = [(sgb, sg_d), (sub, su_d), (sdr, sd_d), (idf, idf_d)]
        if layer == 0:
            consts += [(bkb, bk_d), (bvb, bv_d), (lng, lng_d), (lnb, lnb_d), (bqs, bq_d)]
        else:
            consts += [(bo_bf, bo_d)]
        for t_sb, t_dr in consts:
            p.add("gpsimd", lambda g, a=t_sb, b=t_dr: g.dma_start(out=a[:], in_=b).then_inc(lc.sem, 16))
            lc.inc()
        lc_all = lc.total()
        for buf, val in [(ones_bf, 1.0), (ones_f, 1.0), (ones5f, 1.0), (cneg, -128.0), (ceps, EPS)]:
            p.add("vector", lambda v, b=buf, val=val: v.memset(b[:], val).then_inc(ve, 1))
            cve.inc()
        if layer == 0:
            W(p, "vector", lc.sem, lc_all)
            p.add("vector", lambda v: v.tensor_scalar(
                bqs[:], bqs[:], ISQ, None, ALU.mult).then_inc(ve, 1))
            cve.inc()
        else:
            p.add("vector", lambda v: v.memset(ones_r[:], 1.0).then_inc(ve, 1))
            cve.inc()
        ve_consts = cve.v
        W(p, "tensor", lc.sem, lc_all)
        W(p, "tensor", ve, ve_consts)
        for og in range(4):
            p.add("tensor", lambda t, og=og: t.matmul(
                pb[og][:], ones_f[:], sdr[:, og * 512:(og + 1) * 512], start=True,
                stop=True, skip_group_check=True).then_inc(te, 1))
            te_b = cte.inc()
            W(p, "vector", te, te_b)
            p.add("vector", lambda v, og=og: v.tensor_copy(
                sdb[:, og * 512:(og + 1) * 512], pb[og][:]).then_inc(ve, 1))
            bankfree[og] = ("ve", cve.inc())

        te_grp = {}
        ve_y0 = {}
        for og in range(4):
            for dcg in range(4):
                gi = og * 4 + dcg
                wsl = gi % 2
                if gi >= 2:
                    W(p, "gpsimd", te, te_grp[gi - 2])
                p.add("gpsimd", lambda g, og=og, dcg=dcg, wsl=wsl: g.dma_start(
                    out=wob[:, wsl * 2048:(wsl + 1) * 2048], in_=wo_d[og, dcg]).then_inc(lwo[wsl].sem, 16))
                ld_wo = lwo[wsl].inc()
                W(p, "tensor", lwo[wsl].sem, ld_wo)
                if dcg == 0:
                    for tt in range(8):
                        bwait(tt)
                if dcg < 3:
                    for tt in range(8):
                        for dd in range(4):
                            dc = dcg * 4 + dd
                            def mwo(t, tt=tt, dd=dd, dc=dc, wsl=wsl, last=(tt == 7 and dd == 3)):
                                ins = t.matmul(
                                    pb[tt][:], bb[:, C0 + dc * T + tt * 128: C0 + dc * T + (tt + 1) * 128],
                                    wob[:, wsl * 2048 + dd * 512: wsl * 2048 + (dd + 1) * 512],
                                    start=(dc == 0), stop=False,
                                    skip_group_check=True)
                                if last:
                                    ins.then_inc(te, 1)
                            p.add("tensor", mwo)
                    te_grp[gi] = cte.inc()
                else:
                    for tt in range(8):
                        for dd in range(4):
                            dc = dcg * 4 + dd
                            def mwo2(t, tt=tt, dd=dd, dc=dc, wsl=wsl):
                                ins = t.matmul(
                                    pb[tt][:], bb[:, C0 + dc * T + tt * 128: C0 + dc * T + (tt + 1) * 128],
                                    wob[:, wsl * 2048 + dd * 512: wsl * 2048 + (dd + 1) * 512],
                                    start=False, stop=(dd == 3 and layer == 0),
                                    skip_group_check=True)
                                if dd == 3 and layer == 0:
                                    ins.then_inc(te, 1)
                            p.add("tensor", mwo2)
                        if layer == 1:
                            def mbo(t, og=og, tt=tt):
                                t.matmul(
                                    pb[tt][:], ones_r[:], bo_bf[:, og * 512:(og + 1) * 512],
                                    start=False, stop=True,
                                    skip_group_check=True).then_inc(te, 1)
                            p.add("tensor", mbo)
                        te_tt = cte.inc()
                        W(p, "vector", te, te_tt)
                        p.add("vector", lambda v, og=og, tt=tt: v.tensor_copy(
                            bb[:, A0 + tt * 2048 + og * 512: A0 + tt * 2048 + (og + 1) * 512],
                            pb[tt][:]).then_inc(ve, 1))
                        ve_y0[(og, tt)] = cve.inc()
                        bankfree[tt] = ("ve", ve_y0[(og, tt)])
                    te_grp[gi] = cte.v


        def emit_stats(src_fn, c0, fw_v=(), fw_s=()):
            for w in fw_v:
                gwait("vector", w)
            p.add("vector", lambda v, c0=c0: v.tensor_reduce(
                stat[:, c0:c0 + 1], src_fn(), AX.X, ALU.add).then_inc(ve, 1))
            e_red = cve.inc()
            for w in fw_s:
                gwait("scalar", w)
            p.add("scalar", lambda s, c0=c0: s.activation(
                bb[:, C0:C0 + D], src_fn(), AF.Square,
                accum_out=stat[:, c0 + 2:c0 + 3]).then_inc(se, 1))
            e_sq = cse.inc()
            W(p, "vector", ve, e_red)
            W(p, "vector", se, e_sq)
            p.add("vector", lambda v, c0=c0: v.tensor_scalar(
                stat[:, c0 + 4:c0 + 5], stat[:, c0:c0 + 1], 1.0 / D, None,
                ALU.mult).then_inc(ve, 1))
            e_m = cve.inc()
            W(p, "vector", ve, e_m)
            p.add("vector", lambda v, c0=c0: v.tensor_tensor(
                stat[:, c0 + 6:c0 + 7], stat[:, c0 + 4:c0 + 5], stat[:, c0 + 4:c0 + 5],
                ALU.mult).then_inc(ve, 1))
            e_t = cve.inc()
            W(p, "vector", ve, e_t)
            p.add("vector", lambda v, c0=c0: v.tensor_scalar(
                stat[:, c0 + 8:c0 + 9], stat[:, c0 + 2:c0 + 3], 1.0 / D,
                stat[:, c0 + 6:c0 + 7], ALU.mult, ALU.subtract).then_inc(ve, 1))
            e_var = cve.inc()
            W(p, "scalar", ve, e_var)
            p.add("scalar", lambda s, c0=c0: s.activation(
                stat[:, c0 + 6:c0 + 7], stat[:, c0 + 8:c0 + 9], AF.Ln,
                bias=ceps[:]).then_inc(se, 1))
            e_ln = cse.inc()
            W(p, "scalar", se, e_ln)
            p.add("scalar", lambda s, c0=c0: s.activation(
                stat[:, c0 + 10:c0 + 11], stat[:, c0 + 6:c0 + 7], AF.Exp,
                scale=-0.5).then_inc(se, 1))
            e_rs = cse.inc()
            W(p, "vector", se, e_rs)
            p.add("vector", lambda v, c0=c0: v.tensor_scalar(
                stat[:, c0 + 12:c0 + 13], stat[:, c0 + 4:c0 + 5],
                stat[:, c0 + 10:c0 + 11], -1.0, ALU.mult, ALU.mult).then_inc(ve, 1))
            e_mrs = cve.inc()
            return e_rs, e_mrs, e_var

        so_x1 = {}
        ve_xn1 = {}
        xt_guard = {0: None}
        tile_guard = {0: None, 1: None}
        tstate = {}

        def p2_stageA(tt):
            sl = tt % 2
            st_ = {}
            gwait("gpsimd", xt_guard[0])
            p.add("gpsimd", lambda g, tt=tt: g.dma_start(
                out=xt0[:], in_=x_sl[tt * 128:(tt + 1) * 128, :]).then_inc(lx0.sem, 16))
            st_["ld_x"] = ("lx0", lx0.inc())
            def y0sl(tt=tt):
                return bb[:, A0 + tt * 2048: A0 + (tt + 1) * 2048]
            fw = [("ve", ve_y0[(3, tt)]), tile_guard[sl]]
            fw = [w for w in fw if w is not None]
            _, st_["eA_mrs"], _ = emit_stats(y0sl, sl, fw_v=fw, fw_s=fw)
            tstate[tt] = st_

        def p2_stageB(tt):
            sl = tt % 2
            zb = zb2[sl]
            st_ = tstate[tt]
            W(p, "scalar", ve, st_["eA_mrs"])
            p.add("scalar", lambda s, tt=tt, sl=sl, zb=zb: s.activation(
                zb[:], bb[:, A0 + tt * 2048: A0 + (tt + 1) * 2048], AF.Identity,
                scale=stat[:, sl + 10:sl + 11], bias=stat[:, sl + 12:sl + 13]).then_inc(se, 1))
            eE = cse.inc()
            W(p, "vector", se, eE)
            gwait("vector", st_["ld_x"])
            p.add("vector", lambda v, zb=zb: v.tensor_tensor(
                zb[:], zb[:], xt0[:], ALU.add).then_inc(ve, 1))
            eF = cve.inc()
            xt_guard[0] = ("ve", eF)
            W(p, "sync", ve, eF)
            p.add("sync", lambda sy, tt=tt, zb=zb: sy.dma_start(
                out=x1_scr[tt * 128:(tt + 1) * 128, :], in_=zb[:]).then_inc(sx1.sem, 16))
            so_x1[tt] = sx1.inc()
            _, st_["eB_mrs"], _ = emit_stats(lambda zb=zb: zb[:], 14 + sl,
                                             fw_v=[("ve", eF)], fw_s=[("ve", eF)])

        def p2_stageC(tt):
            sl = tt % 2
            zb = zb2[sl]
            st_ = tstate[tt]
            W(p, "scalar", sx1.sem, so_x1[tt])
            W(p, "scalar", ve, st_["eB_mrs"])
            p.add("scalar", lambda s, sl=sl, zb=zb: s.activation(
                zb[:], zb[:], AF.Identity, scale=stat[:, sl + 24:sl + 25],
                bias=stat[:, sl + 26:sl + 27]).then_inc(se, 1))
            eJ = cse.inc()
            W(p, "tensor", se, eJ)
            te_tr_last = 0
            for dc in range(16):
                bk = dc % 2
                bwait(bk)
                p.add("tensor", lambda t, dc=dc, bk=bk, zb=zb: t.transpose(
                    pb[bk][:, 0:128], zb[:, dc * 128:(dc + 1) * 128],
                    idf[:]).then_inc(te, 1))
                te_tr = cte.inc()
                te_tr_last = te_tr
                W(p, "vector", te, te_tr)
                p.add("vector", lambda v, tt=tt, dc=dc, bk=bk: v.tensor_copy(
                    bb[:, B0 + dc * T + tt * 128: B0 + dc * T + (tt + 1) * 128],
                    pb[bk][:, 0:128]).then_inc(ve, 1))
                ve_e = cve.inc()
                ve_xn1[(tt, dc)] = ve_e
                bankfree[bk] = ("ve", ve_e)
            tile_guard[sl] = ("te", te_tr_last)
            del tstate[tt]

        for step in range(10):
            if 2 <= step:
                p2_stageC(step - 2)
            if 1 <= step <= 8:
                p2_stageB(step - 1)
            if step < 8:
                p2_stageA(step)
        ve_xn1_all = ve_xn1[(7, 15)]

        so_x2 = {}
        xs_guard = None
        hs_guard = None
        y1_scalar_guard = None
        gu_slot_guard = {0: None, 1: None}
        gu_gate_guard = {0: None, 1: None}
        wd_slot_guard = {0: None, 1: None}
        for tg in range(2):
            bwait(6)
            W(p, "tensor", ve, ve_xn1[(3 if tg == 0 else 7, 15)])
            for hc in range(16):
                def mxs(t, hc=hc, tg=tg):
                    ins = t.matmul(
                        pb[6][0:1, :], ones_bf[:],
                        bb[:, B0 + hc * T + tg * 512: B0 + hc * T + tg * 512 + 512],
                        start=(hc == 0), stop=(hc == 15), skip_group_check=True)
                    if hc == 15:
                        ins.then_inc(te, 1)
                p.add("tensor", mxs)
            te_xs = cte.inc()
            W(p, "vector", te, te_xs)
            gwait("vector", xs_guard)
            p.add("vector", lambda v: v.tensor_copy(xs_bf[:], pb[6][0:1, :]).then_inc(ve, 1))
            ve_xs = cve.inc()
            bankfree[6] = ("ve", ve_xs)

            te_gu = {}
            se_sil = {}
            ve_ue = {}
            ve_h = {}
            for ic in range(IC):
                wsl = ic % 2
                gwait("gpsimd", gu_gate_guard[wsl])
                p.add("gpsimd", lambda g, ic=ic, wsl=wsl: g.dma_start(
                    out=wgub[wsl][:, 0:2048], in_=wgu_d[ic, :, 0:2048]).then_inc(lgu[wsl].sem, 16))
                ld_wg2 = lgu[wsl].inc()
                gwait("gpsimd", gu_slot_guard[wsl])
                p.add("gpsimd", lambda g, ic=ic, wsl=wsl: g.dma_start(
                    out=wgub[wsl][:, 2048:4096], in_=wgu_d[ic, :, 2048:4096]).then_inc(lguB[wsl].sem, 16))
                ld_wu = lguB[wsl].inc()
                W(p, "tensor", lgu[wsl].sem, ld_wg2)
                W(p, "tensor", ve, ve_xs)
                bwait(4)
                bwait(5)
                for hc in range(16):
                    p.add("tensor", lambda t, ic=ic, hc=hc, tg=tg, wsl=wsl: t.matmul(
                        pb[4][:], wgub[wsl][:, hc * 128:(hc + 1) * 128],
                        bb[:, B0 + hc * T + tg * 512: B0 + hc * T + tg * 512 + 512],
                        start=(hc == 0), stop=False, skip_group_check=True))
                p.add("tensor", lambda t: t.matmul(
                    pb[4][:], cneg[:], xs_bf[:], start=False, stop=True,
                    skip_group_check=True).then_inc(te, 1))
                te_g = cte.inc()
                W(p, "tensor", lguB[wsl].sem, ld_wu)
                for hc in range(16):
                    p.add("tensor", lambda t, ic=ic, hc=hc, tg=tg, wsl=wsl: t.matmul(
                        pb[5][:], wgub[wsl][:, 2048 + hc * 128: 2048 + (hc + 1) * 128],
                        bb[:, B0 + hc * T + tg * 512: B0 + hc * T + tg * 512 + 512],
                        start=(hc == 0), stop=False, skip_group_check=True))
                p.add("tensor", lambda t: t.matmul(
                    pb[5][:], cneg[:], xs_bf[:], start=False, stop=True,
                    skip_group_check=True).then_inc(te, 1))
                te_gu[ic] = cte.inc()
                gu_gate_guard[wsl] = ("te", te_g)
                gu_slot_guard[wsl] = ("te", te_gu[ic])
                xs_guard = ("te", te_gu[ic])
                W(p, "scalar", te, te_g)
                if ic >= 1:
                    W(p, "scalar", ve, ve_h[ic - 1])
                elif tg == 1:
                    W(p, "scalar", ve, ve_h_prevtg)
                p.add("scalar", lambda s, ic=ic: s.activation(
                    sgt, pb[4][:], AF.Silu, scale=sgb[:, ic:ic + 1]).then_inc(se, 1))
                se_sil[ic] = cse.inc()
                bankfree[4] = ("se", se_sil[ic])
                W(p, "vector", te, te_gu[ic])
                if ic >= 1:
                    W(p, "vector", ve, ve_h[ic - 1])
                elif tg == 1:
                    W(p, "vector", ve, ve_h_prevtg)
                p.add("vector", lambda v, ic=ic: v.tensor_scalar(
                    ut, pb[5][:], sub[:, ic:ic + 1], None, ALU.mult).then_inc(ve, 1))
                ve_ue[ic] = cve.inc()
                bankfree[5] = ("ve", ve_ue[ic])
                W(p, "vector", se, se_sil[ic])
                W(p, "vector", ve, ve_ue[ic])
                p.add("vector", lambda v, ic=ic: v.tensor_tensor(
                    bb[:, C0 + ic * 512: C0 + (ic + 1) * 512], sgt, ut,
                    ALU.mult).then_inc(ve, 1))
                ve_h[ic] = cve.inc()
            ve_h_all = ve_h[IC - 1]
            ve_h_prevtg = ve_h_all

            bwait(7)
            W(p, "tensor", ve, ve_h_all)
            for ic in range(IC):
                def mhs(t, ic=ic):
                    ins = t.matmul(
                        pb[7][0:1, :], ones_bf[:], bb[:, C0 + ic * 512: C0 + (ic + 1) * 512],
                        start=(ic == 0), stop=(ic == IC - 1), skip_group_check=True)
                    if ic == IC - 1:
                        ins.then_inc(te, 1)
                p.add("tensor", mhs)
            te_hs = cte.inc()
            W(p, "vector", te, te_hs)
            gwait("vector", hs_guard)
            p.add("vector", lambda v: v.tensor_copy(hs_f[:], pb[7][0:1, :]).then_inc(ve, 1))
            ve_hs = cve.inc()
            W(p, "vector", ve, ve_hs)
            p.add("vector", lambda v: v.tensor_scalar(
                hs_f[:], hs_f[:], -128.0, None, ALU.mult).then_inc(ve, 1))
            ve_hsn = cve.inc()
            bankfree[7] = ("ve", ve_hsn)

            te_wd = {}
            ve_y1 = {}
            te_corr_done = 0
            for ogp in range(2):
                for ic in range(IC):
                    wsl = (ic // 2) % 2
                    if ic % 2 == 0:
                        gwait("gpsimd", wd_slot_guard[wsl])
                        p.add("gpsimd", lambda g, ogp=ogp, ic=ic, wsl=wsl: g.dma_start(
                            out=wdb[wsl][:], in_=wd_d[ogp, ic // 2]).then_inc(lwd[wsl].sem, 16))
                        ld_wd = lwd[wsl].inc()
                    W(p, "tensor", lwd[wsl].sem, ld_wd)
                    if ic == 0:
                        W(p, "tensor", ve, ve_hsn)
                        for bk in range(8):
                            bwait(bk)
                    for ts_ in range(4):
                        for o2 in range(2):
                            bk = ts_ * 2 + o2
                            def mdn(t, ic=ic, ts_=ts_, o2=o2, bk=bk, wsl=wsl,
                                    last=(ts_ == 3 and o2 == 1)):
                                ins = t.matmul(
                                    pb[bk][:],
                                    bb[:, C0 + ic * 512 + ts_ * 128: C0 + ic * 512 + (ts_ + 1) * 128],
                                    wdb[wsl][:, (ic % 2) * 1024 + o2 * 512: (ic % 2) * 1024 + (o2 + 1) * 512],
                                    start=(ic == 0), stop=False, skip_group_check=True)
                                if last:
                                    ins.then_inc(te, 1)
                            p.add("tensor", mdn)
                    te_wd[(ogp, ic)] = cte.inc()
                    wd_slot_guard[wsl] = ("te", te_wd[(ogp, ic)])
                for ts_ in range(4):
                    for o2 in range(2):
                        bk = ts_ * 2 + o2
                        og = ogp * 2 + o2
                        def mcor(t, ts_=ts_, bk=bk):
                            t.matmul(
                                pb[bk][:], hs_f[:, ts_ * 128:(ts_ + 1) * 128],
                                ones5f[:], start=False, stop=True,
                                skip_group_check=True).then_inc(te, 1)
                        p.add("tensor", mcor)
                        te_corr = cte.inc()
                        W(p, "vector", te, te_corr)
                        gwait("vector", y1_scalar_guard)
                        p.add("vector", lambda v, ts_=ts_, og=og, bk=bk: v.tensor_tensor(
                            bb[:, A0 + ts_ * 2048 + og * 512: A0 + ts_ * 2048 + (og + 1) * 512],
                            pb[bk][:], sdb[:, og * 512:(og + 1) * 512], ALU.mult).then_inc(ve, 1))
                        ve_y1[(ts_, og)] = cve.inc()
                        bankfree[bk] = ("ve", ve_y1[(ts_, og)])
                hs_guard = ("te", cte.v)

            p4state = {}

            def p4_stageA(ts_, tg=tg):
                tt = tg * 4 + ts_
                sl = tt % 2
                st_ = {}
                if ts_ == 0:
                    W(p, "gpsimd", sx1.sem, sx1.total())
                gwait("gpsimd", xt_guard[0])
                p.add("gpsimd", lambda g, tt=tt: g.dma_start(
                    out=xt0[:], in_=x1_scr[tt * 128:(tt + 1) * 128, :]).then_inc(lx0.sem, 16))
                st_["ld_x"] = ("lx0", lx0.inc())
                def y1sl(ts_=ts_):
                    return bb[:, A0 + ts_ * 2048: A0 + (ts_ + 1) * 2048]
                fw = [("ve", ve_y1[(ts_, og)]) for og in range(4)]
                if tile_guard[sl] is not None:
                    fw = fw + [tile_guard[sl]]
                _, st_["eA_mrs"], _ = emit_stats(y1sl, sl, fw_v=fw, fw_s=fw)
                p4state[ts_] = st_

            def p4_stageB(ts_, tg=tg):
                tt = tg * 4 + ts_
                sl = tt % 2
                zb = zb2[sl]
                st_ = p4state[ts_]
                W(p, "scalar", ve, st_["eA_mrs"])
                p.add("scalar", lambda s, ts_=ts_, sl=sl, zb=zb: s.activation(
                    zb[:], bb[:, A0 + ts_ * 2048: A0 + (ts_ + 1) * 2048], AF.Identity,
                    scale=stat[:, sl + 10:sl + 11], bias=stat[:, sl + 12:sl + 13]).then_inc(se, 1))
                eE = cse.inc()
                W(p, "vector", se, eE)
                gwait("vector", st_["ld_x"])
                p.add("vector", lambda v, zb=zb: v.tensor_tensor(
                    zb[:], zb[:], xt0[:], ALU.add).then_inc(ve, 1))
                ve_x2 = cve.inc()
                xt_guard[0] = ("ve", ve_x2)
                W(p, "sync", ve, ve_x2)
                p.add("sync", lambda sy, tt=tt, zb=zb: sy.dma_start(
                    out=x_out[tt * 128:(tt + 1) * 128, :], in_=zb[:]).then_inc(sxo.sem, 16))
                so_x2[tt] = sxo.inc()
                tile_guard[sl] = ("sxo", so_x2[tt])
                return eE

            eE_last = 0
            for step in range(5):
                if 1 <= step:
                    eE_last = p4_stageB(step - 1)
                if step < 4:
                    p4_stageA(step)
            y1_scalar_guard = ("se", eE_last)

        if layer == 1:
            W(p, "sync", sxo.sem, sxo.total())
            p.run(nc)
            return nc

        ve_xn2 = {}
        bstate = {}

        def p4b_stageA(tt):
            sl = tt % 2
            st_ = {}
            if tt == 0:
                W(p, "gpsimd", sxo.sem, sxo.total())
            gwait("gpsimd", xt_guard[0])
            p.add("gpsimd", lambda g, tt=tt: g.dma_start(
                out=xt0[:], in_=x_out[tt * 128:(tt + 1) * 128, :]).then_inc(lx0.sem, 16))
            ld_x2 = ("lx0", lx0.inc())
            fw = [ld_x2]
            if tile_guard[sl] is not None:
                fw = fw + [tile_guard[sl]]
            eA_rs, _, _ = emit_stats(lambda: xt0[:], sl, fw_v=fw, fw_s=fw)
            W(p, "vector", se, eA_rs)
            p.add("vector", lambda v, sl=sl: v.tensor_scalar(
                stat[:, 28 + sl:29 + sl], stat[:, 8 + sl:9 + sl],
                stat[:, 10 + sl:11 + sl], stat[:, 10 + sl:11 + sl], ALU.mult,
                ALU.mult).then_inc(ve, 1))
            ve_w = cve.inc()
            W(p, "scalar", ve, ve_w)
            p.add("scalar", lambda s, sl=sl: s.activation(
                stat[:, 30 + sl:31 + sl], stat[:, 28 + sl:29 + sl], AF.Ln,
                bias=ceps[:]).then_inc(se, 1))
            se_l2 = cse.inc()
            W(p, "scalar", se, se_l2)
            p.add("scalar", lambda s, sl=sl: s.activation(
                stat[:, 30 + sl:31 + sl], stat[:, 30 + sl:31 + sl], AF.Exp,
                scale=-0.5).then_inc(se, 1))
            se_rb = cse.inc()
            W(p, "vector", se, se_rb)
            p.add("vector", lambda v, sl=sl: v.tensor_tensor(
                stat[:, 32 + sl:33 + sl], stat[:, 10 + sl:11 + sl],
                stat[:, 30 + sl:31 + sl], ALU.mult).then_inc(ve, 1))
            ve_r12 = cve.inc()
            W(p, "vector", ve, ve_r12)
            p.add("vector", lambda v, sl=sl: v.tensor_scalar(
                stat[:, 34 + sl:35 + sl], stat[:, 4 + sl:5 + sl],
                stat[:, 32 + sl:33 + sl], -1.0, ALU.mult, ALU.mult).then_inc(ve, 1))
            st_["ve_mr12"] = cve.inc()
            bstate[tt] = st_

        def p4b_stageB(tt):
            sl = tt % 2
            zb = zb2[sl]
            st_ = bstate[tt]
            W(p, "scalar", ve, st_["ve_mr12"])
            p.add("scalar", lambda s, sl=sl, zb=zb: s.activation(
                zb[:], xt0[:], AF.Identity, scale=stat[:, 32 + sl:33 + sl],
                bias=stat[:, 34 + sl:35 + sl]).then_inc(se, 1))
            eEp = cse.inc()
            xt_guard[0] = ("se", eEp)
            W(p, "tensor", se, eEp)
            te_tr_last = 0
            for dc in range(16):
                bk = dc % 2
                bwait(bk)
                p.add("tensor", lambda t, dc=dc, bk=bk, zb=zb: t.transpose(
                    pb[bk][:, 0:128], zb[:, dc * 128:(dc + 1) * 128],
                    idf[:]).then_inc(te, 1))
                te_tr = cte.inc()
                te_tr_last = te_tr
                W(p, "vector", te, te_tr)
                p.add("vector", lambda v, tt=tt, dc=dc, bk=bk: v.tensor_scalar(
                    bb[:, B0 + dc * T + tt * 128: B0 + dc * T + (tt + 1) * 128],
                    pb[bk][:, 0:128], lng[:, dc:dc + 1], lnb[:, dc:dc + 1],
                    ALU.mult, ALU.add).then_inc(ve, 1))
                ve_e = cve.inc()
                ve_xn2[(tt, dc)] = ve_e
                bankfree[bk] = ("ve", ve_e)
            tile_guard[sl] = ("te", te_tr_last)
            del bstate[tt]

        for step in range(9):
            if 1 <= step:
                p4b_stageB(step - 1)
            if step < 8:
                p4b_stageA(step)
        ve_xn2_all = ve_xn2[(7, 15)]

        te_qk = {}
        ve_evac = {}
        so_qk = {}
        for oc in range(48):
            wsl = oc % 2
            if oc >= 2:
                W(p, "gpsimd", te, te_qk[oc - 2])
            p.add("gpsimd", lambda g, oc=oc, wsl=wsl: g.dma_start(
                out=wab[wsl], in_=wa1_d[oc]).then_inc(lwa[wsl].sem, 16))
            ld_w = lwa[wsl].inc()
            for tg in range(2):
                bank = (2 * oc + tg) % 4
                W(p, "tensor", lwa[wsl].sem, ld_w)
                if oc == 0:
                    W(p, "tensor", ve, ve_xn2[(3 if tg == 0 else 7, 15)])
                bwait(bank)
                for ic in range(16):
                    def mq(t, oc=oc, tg=tg, ic=ic, bank=bank, wsl=wsl):
                        ins = t.matmul(
                            pb[bank][:], bb[:, A0 + wsl * 2048 + ic * 128: A0 + wsl * 2048 + (ic + 1) * 128],
                            bb[:, B0 + ic * T + tg * 512: B0 + ic * T + tg * 512 + 512],
                            start=(ic == 0), stop=(ic == 15), skip_group_check=True)
                        if ic == 15:
                            ins.then_inc(te, 1)
                    p.add("tensor", mq)
                te_mm = cte.inc()
                if tg == 1:
                    te_qk[oc] = te_mm
                W(p, "vector", te, te_mm)
                if oc >= 2 and tg == 0:
                    W(p, "vector", sst[wsl].sem, so_qk[oc - 2])
                kind = oc // 16
                off = oc % 16
                if kind == 0:
                    p.add("vector", lambda v, oc=oc, tg=tg, bank=bank, off=off: v.tensor_scalar(
                        bb[:, A0 + 4096 + (oc % 2) * T + tg * 512: A0 + 4096 + (oc % 2) * T + tg * 512 + 512],
                        pb[bank][:], ISQ,
                        bqs[:, off:off + 1], ALU.mult, ALU.add).then_inc(ve, 1))
                else:
                    bias = bkb if kind == 1 else bvb
                    p.add("vector", lambda v, oc=oc, tg=tg, bank=bank, off=off, bias=bias: v.tensor_scalar(
                        bb[:, A0 + 4096 + (oc % 2) * T + tg * 512: A0 + 4096 + (oc % 2) * T + tg * 512 + 512],
                        pb[bank][:],
                        bias[:, off:off + 1], None, ALU.add).then_inc(ve, 1))
                ve_evac[(oc, tg)] = cve.inc()
                bankfree[bank] = ("ve", ve_evac[(oc, tg)])
            W(p, "sync", ve, ve_evac[(oc, 1)])
            dst = (q1_o, k1_o, v1_o)[oc // 16]
            r0 = (oc % 16) * 128
            p.add("sync", lambda sy, oc=oc, dst=dst, r0=r0, wsl=wsl: sy.dma_start(
                out=dst[r0:r0 + 128, :], in_=stg[oc % 2]).then_inc(sst[wsl].sem, 16))
            so_qk[oc] = sst[wsl].inc()

        for ss in (sst[0], sst[1], sxo):
            W(p, "sync", ss.sem, ss.total())
        p.run(nc)
    return nc


MASKS_BF = None


def _masks_bf():
    global MASKS_BF
    if MASKS_BF is None:
        m = np.zeros((128, 4 * 512), np.float32)
        ki = np.arange(128)[:, None]
        qi = np.arange(512)[None, :]
        for j in range(4):
            m[:, j * 512:(j + 1) * 512] = np.where(j * 128 + ki <= qi, 0.0, NEG)
        MASKS_BF = m.astype(BF)
    return MASKS_BF


IDENT_F32 = np.eye(128, dtype=np.float32)


def _tile_oc(Wmat):
    n = Wmat.shape[1] // 128
    return np.ascontiguousarray(
        Wmat.reshape(16, 128, n, 128).transpose(2, 1, 0, 3).reshape(n, 128, 2048))


def _mid_weights(layer, inputs):
    if layer == 0:
        Wo = inputs["L0_attn_W_o"]
        wg, wu, wd = inputs["L0_ffn_gate_w"], inputs["L0_ffn_up_w"], inputs["L0_ffn_down_w"]
        sg, su, sd = inputs["L0_ffn_gate_scale"], inputs["L0_ffn_up_scale"], inputs["L0_ffn_down_scale"]
    else:
        Wo = inputs["L1_attn_Wo"]
        wg, wu, wd = inputs["L1_ffn_wq_gate"], inputs["L1_ffn_wq_up"], inputs["L1_ffn_wq_down"]
        sg, su, sd = inputs["L1_ffn_s_gate"], inputs["L1_ffn_s_up"], inputs["L1_ffn_s_down"]
    IC = wg.shape[0] // 128
    wo_t = np.ascontiguousarray(
        Wo.reshape(16, 128, 4, 512).transpose(2, 0, 1, 3).reshape(4, 4, 4, 128, 512)
        .transpose(0, 1, 3, 2, 4).reshape(4, 4, 128, 2048))
    wg_t = wg.reshape(IC, 128, 16, 128).transpose(0, 3, 2, 1).reshape(IC, 128, 2048)
    wu_t = wu.reshape(IC, 128, 16, 128).transpose(0, 3, 2, 1).reshape(IC, 128, 2048)
    wgu_t = np.ascontiguousarray(np.concatenate([wg_t, wu_t], axis=2))
    wd_t = np.ascontiguousarray(
        wd.T.reshape(IC // 2, 2, 128, 2, 1024).transpose(3, 0, 2, 1, 4).reshape(2, IC // 2, 128, 2048))
    d = {
        "wo_d": wo_t,
        "wgu_d": wgu_t,
        "wd_d": wd_t,
        "sg_d": np.ascontiguousarray(sg.reshape(IC, 128).T),
        "su_d": np.ascontiguousarray(su.reshape(IC, 128).T),
        "sd_d": np.ascontiguousarray(sd.reshape(1, D)),
        "idf_d": IDENT_F32,
    }
    if layer == 0:
        wa1 = np.concatenate([
            _tile_oc(inputs["L1_attn_Wq"]),
            _tile_oc(inputs["L1_attn_Wk"]),
            _tile_oc(inputs["L1_attn_Wv"])], axis=0)
        d["wa1_d"] = np.ascontiguousarray(wa1)
        for nm, key in [("bq_d", "L1_attn_bq"), ("bk_d", "L1_attn_bk"),
                        ("bv_d", "L1_attn_bv"), ("lng_d", "L1_attn_ln_g"),
                        ("lnb_d", "L1_attn_lnb" if False else "L1_attn_ln_b")]:
            d[nm] = np.ascontiguousarray(inputs[key].reshape(16, 128).T)
    else:
        d["bo_d"] = np.ascontiguousarray(inputs["L1_attn_bo"].reshape(1, D))
    return d


def _attn_in_maps(q_res, k_res, v_res_tm=None, v_res_fm=None):
    masks = _masks_bf()
    in_maps = []
    for ac in range(NCORES):
        b = ac // 4
        heads = [4 * (ac % 4) + i for i in range(NPAIR)]
        qs = np.empty((NPAIR, 128, S), BF)
        kk = np.empty((NPAIR, 128, S), BF)
        vt = np.empty((NPAIR, 128, S), BF)
        for i, h in enumerate(heads):
            qs[i] = np.concatenate([q_res[4 * b + j][h * 128:(h + 1) * 128, :] for j in range(4)], axis=1)
            kk[i] = np.concatenate([k_res[4 * b + j][h * 128:(h + 1) * 128, :] for j in range(4)], axis=1)
            if v_res_tm is not None:
                vfull = np.concatenate([v_res_tm[4 * b + j][:, h * 128:(h + 1) * 128] for j in range(4)], axis=0)
            else:
                vfull = np.concatenate([v_res_fm[4 * b + j][h * 128:(h + 1) * 128, :] for j in range(4)], axis=1).T
            vt[i] = vfull.reshape(32, 128, 128).transpose(1, 0, 2).reshape(128, S)
        in_maps.append({"qs_d": np.ascontiguousarray(qs), "kk_d": np.ascontiguousarray(kk),
                        "vt_d": np.ascontiguousarray(vt), "mk_d": masks, "id_d": IDENT_BF})
    return in_maps


def _av_for_core(avn_res, c):
    b = c // 4
    sl = slice((c % 4) * T, (c % 4 + 1) * T)
    out = np.empty((16, 128, T), BF)
    for h in range(16):
        ac = 4 * b + h // 4
        out[h] = avn_res[ac][h % 4][:, sl]
    return out


def get_prog(name):
    if name not in _cache:
        if name == "l1":
            _cache[name] = build_l1()
        elif name == "l2":
            _cache[name] = build_l2()
        elif name == "mid0":
            _cache[name] = build_mid(0)
        elif name == "mid1":
            _cache[name] = build_mid(1)
    return _cache[name]


def kernel(**inputs):
    x = np.asarray(inputs["L0_attn_x"])
    xt = np.ascontiguousarray(x.reshape(NCORES, T, D))

    wa, wv = l1_weight_layout(np.asarray(inputs["L0_attn_W_qkv"]))
    nc1 = get_prog("l1")
    im1 = [{"x_sl": xt[c], "wa": wa, "wv": wv, "ident_in": IDENT_BF}
           for c in range(NCORES)]
    r1 = run_bass_kernel_spmd(nc1, im1, core_ids=list(range(NCORES))).results

    nc2 = get_prog("l2")
    im2 = _attn_in_maps([r["q_out"] for r in r1], [r["k_out"] for r in r1],
                        v_res_tm=[r["v_out"] for r in r1])
    r2 = run_bass_kernel_spmd(nc2, im2, core_ids=list(range(NCORES))).results
    avn0 = [r["avn_d"] for r in r2]

    w0 = _mid_weights(0, inputs)
    nc3 = get_prog("mid0")
    im3 = []
    for c in range(NCORES):
        m = dict(w0)
        m["x_sl"] = xt[c]
        m["av_d"] = _av_for_core(avn0, c)
        im3.append(m)
    r3 = run_bass_kernel_spmd(nc3, im3, core_ids=list(range(NCORES))).results

    im4 = _attn_in_maps([r["q1_o"] for r in r3], [r["k1_o"] for r in r3],
                        v_res_fm=[r["v1_o"] for r in r3])
    r4 = run_bass_kernel_spmd(nc2, im4, core_ids=list(range(NCORES))).results
    avn1 = [r["avn_d"] for r in r4]

    w1 = _mid_weights(1, inputs)
    nc5 = get_prog("mid1")
    im5 = []
    for c in range(NCORES):
        m = dict(w1)
        m["x_sl"] = r3[c]["x_out"]
        m["av_d"] = _av_for_core(avn1, c)
        im5.append(m)
    r5 = run_bass_kernel_spmd(nc5, im5, core_ids=list(range(NCORES))).results

    out = np.concatenate([r5[c]["x_out"] for c in range(NCORES)], axis=0)
    return out.reshape(B, S, D)
